# version 10
# baseline (speedup 1.0000x reference)
import numpy as np
import ml_dtypes
import concourse.bass as bass
import concourse.mybir as mybir
from concourse.bass_utils import run_bass_kernel_spmd
from contextlib import ExitStack

F32 = mybir.dt.float32
BF16 = mybir.dt.bfloat16
AF = mybir.ActivationFunctionType
ALU = mybir.AluOpType

_DEBUG = 0
_PHASES = 6
_LAST = {}

SEQ, D, NCORES = 8192, 4096, 8
ROWS = SEQ // NCORES
HD, NH, NKV = 128, 32, 8
DF, NJT, RANK = 11008, 86, 16
BLOCK, LOCAL, VERT = 64, 16, 8
NVSEG, NSEG2 = 16, 32
SKV = (NVSEG + NSEG2) * BLOCK
NEG = -1e30
SM = 1.0 / HD
NDT = D // 128
LU = 20
NSB = SKV // 512
MASK_UNITS = [8, 9, 10, 11, 16, 17, 18, 19]
MASK_SLOT = {u: s for s, u in enumerate(MASK_UNITS)}


def _verts_real(c):
    vbase = max(0, 16 * c - 15)
    verts = [j for j in range(vbase) if (j + 1) % VERT == 0]
    if c > 0:
        stray = [j for j in range(vbase, vbase + 8) if (j + 1) % VERT == 0]
        assert len(stray) == 1
        verts += stray
    return verts, vbase


def _packed_blocks(c):
    verts, vbase = _verts_real(c)
    vseg = verts + [-1] * (NVSEG - len(verts))
    seg2 = [b if b >= 0 else -1 for b in range(16 * c - 16, 16 * c + 16)]
    return vseg + seg2, len(verts), vbase


def _unit_kt(qt, u):
    return u if u < 8 else 8 + (u - 8) + 4 * qt


def _bias_pattern(c):
    packed, _, vbase = _packed_blocks(c)
    verts = packed[:NVSEG]
    pat = np.zeros((128, 2 * LU), dtype=np.float32)
    for qt in range(2):
        i0 = 16 * c + 8 * qt
        span_start = max(vbase, i0 - 15)
        span_end = i0 + 8
        nv_t = sum(1 for j in verts if 0 <= j < span_start)
        for u in range(LU):
            kt = _unit_kt(qt, u)
            col = np.full(128, NEG, dtype=np.float32)
            for half in range(2):
                pos = 2 * kt + half
                kb = packed[pos]
                if kb < 0:
                    continue
                ok = (pos < nv_t) if u < 8 else (span_start <= kb < span_end)
                if ok:
                    col[half * 64:(half + 1) * 64] = 0.0
            pat[:, qt * LU + u] = col
    return pat


def _mask_consts():
    m = np.zeros((128, 8 * 512), dtype=np.float32)
    for s in range(4):
        t = np.full((128, 512), NEG, dtype=np.float32)
        for half in range(2):
            j = 2 * s + half - 1
            if j < 0:
                continue
            for qb in range(8):
                if qb <= j or j % 8 == 6:
                    t[half * 64:(half + 1) * 64, qb * 64:(qb + 1) * 64] = 0.0
        m[:, s * 512:(s + 1) * 512] = t
    for dm in range(4):
        t = np.full((128, 512), NEG, dtype=np.float32)
        for half in range(2):
            kbrel = 2 * dm + half
            for qb in range(8):
                sub = t[half * 64:(half + 1) * 64, qb * 64:(qb + 1) * 64]
                if qb > kbrel:
                    sub[:] = 0.0
                elif qb == kbrel:
                    r = np.arange(64)
                    sub[:] = np.where(r[None, :] >= r[:, None], 0.0, NEG)
        m[:, (4 + dm) * 512:(5 + dm) * 512] = t
    return m


def _rope_tables(positions):
    inv = (np.float32(1.0) / (np.float32(1e6) **
           (np.arange(0, HD, 2, dtype=np.float32) / np.float32(HD)))).astype(np.float32)
    t = np.asarray(positions, dtype=np.float32)
    fr = np.outer(inv, t).astype(np.float32)
    return (np.concatenate([np.cos(fr), np.cos(fr)], 0).astype(np.float32),
            np.concatenate([-np.sin(fr), np.sin(fr)], 0).astype(np.float32))


def _pack_lhsT(WT, n_kt, n_mt):
    a = WT.reshape(n_kt, 128, n_mt, 128)
    return np.ascontiguousarray(np.transpose(a, (2, 1, 0, 3)))


def _pack_inputs(inputs):
    x = np.asarray(inputs["L0_attn_hidden_states"])[0]
    xT = np.ascontiguousarray(x.T)
    qrows = np.concatenate([np.arange(g * 768, g * 768 + 512) for g in range(NKV)])
    krows = np.concatenate([np.arange(g * 768 + 512, g * 768 + 640) for g in range(NKV)])
    vrows = np.concatenate([np.arange(g * 768 + 640, g * 768 + 768) for g in range(NKV)])
    w_qkv = np.asarray(inputs["L0_attn_w_qkv"])
    b_qkv = np.asarray(inputs["L0_attn_b_qkv"])
    sh = {}
    sh["wq_pk"] = _pack_lhsT(np.ascontiguousarray(w_qkv[qrows].T), NDT, NH)
    sh["wk_pk"] = _pack_lhsT(np.ascontiguousarray(w_qkv[krows].T), NDT, NKV)
    a = np.ascontiguousarray(w_qkv[vrows].T).reshape(NDT, 128, 2, 512)
    sh["wv_pk"] = np.ascontiguousarray(np.transpose(a, (2, 1, 0, 3)))
    sh["wg_pk"] = _pack_lhsT(np.ascontiguousarray(np.asarray(inputs["L0_ffn_gate_w"]).T), NDT, NJT)
    sh["wu_pk"] = _pack_lhsT(np.ascontiguousarray(np.asarray(inputs["L0_ffn_up_w"]).T), NDT, NJT)
    a = np.ascontiguousarray(np.asarray(inputs["L0_attn_w_dense"]).T).reshape(NDT, 128, 8, 4, 128)
    a = np.transpose(a, (2, 0, 1, 3, 4)).reshape(8, 4, 8, 128, 512)
    sh["wd_pk"] = np.ascontiguousarray(np.transpose(a, (0, 1, 3, 2, 4)))
    a = np.ascontiguousarray(np.asarray(inputs["L0_ffn_down_w"]).T).reshape(NJT, 128, 8, 4, 128)
    a = np.transpose(a, (2, 0, 1, 3, 4)).reshape(8, NJT, 128, 512)
    pad = np.zeros((8, 88 - NJT, 128, 512), dtype=a.dtype)
    a = np.concatenate([a, pad], axis=1).reshape(8, 22, 4, 128, 512)
    sh["wn_pk"] = np.ascontiguousarray(np.transpose(a, (0, 1, 3, 2, 4)))
    a = np.ascontiguousarray(np.asarray(inputs["L0_ffn_gate_a"]).T).reshape(NDT, 128, RANK)
    sh["ga_pk"] = np.ascontiguousarray(np.transpose(a, (1, 0, 2)))
    a = np.ascontiguousarray(np.asarray(inputs["L0_ffn_up_a"]).T).reshape(NDT, 128, RANK)
    sh["ua_pk"] = np.ascontiguousarray(np.transpose(a, (1, 0, 2)))
    sh["gbT"] = np.ascontiguousarray(np.asarray(inputs["L0_ffn_gate_b"]).T)
    sh["ubT"] = np.ascontiguousarray(np.asarray(inputs["L0_ffn_up_b"]).T)
    a = np.ascontiguousarray(np.asarray(inputs["L0_ffn_down_a"]).T).reshape(NJT, 128, RANK)
    sh["da_pk"] = np.ascontiguousarray(np.transpose(a, (1, 0, 2)))
    sh["dbT"] = np.ascontiguousarray(np.asarray(inputs["L0_ffn_down_b"]).T)
    sh["bq_pk"] = np.ascontiguousarray(b_qkv[qrows].reshape(NH, 128).T)
    sh["bk_pk"] = np.ascontiguousarray(b_qkv[krows].reshape(NKV, 128).T)
    sh["bv_pk"] = np.ascontiguousarray(b_qkv[vrows].reshape(1, 1024))
    sh["bd_pk"] = np.ascontiguousarray(np.asarray(inputs["L0_attn_b_dense"]).reshape(1, D))
    sh["mask_pk"] = _mask_consts()
    sh["ones_col"] = np.ones((128, 1), dtype=np.float32)
    sh["ones_colb"] = np.ones((128, 1), dtype=ml_dtypes.bfloat16)
    sh["ones_row"] = np.ones((1, 512), dtype=np.float32)

    in_maps = []
    for c in range(NCORES):
        packed, _, _ = _packed_blocks(c)
        colidx = np.concatenate([np.arange(max(b, 0) * 64, max(b, 0) * 64 + 64)
                                 for b in packed])
        m = dict(sh)
        m["xpk"] = np.ascontiguousarray(xT[:, colidx])
        m["cosk"], m["sink"] = _rope_tables(colidx)
        m["coso"], m["sino"] = _rope_tables(np.arange(ROWS * c, ROWS * (c + 1)))
        m["biaspat"] = _bias_pattern(c)
        in_maps.append(m)
    return in_maps


class _Sem:
    __slots__ = ("h", "n")

    def __init__(self, ctx, nc, name):
        self.h = ctx.enter_context(nc.semaphore(name))
        self.n = 0

    def inc(self, instr, k=1):
        instr.then_inc(self.h, k)
        self.n += k
        return instr


def _build(nc):
    dt = nc.dram_tensor
    sk = "ExternalOutput" if _DEBUG else "Internal"
    xpk = dt("xpk", [D, SKV], F32, kind="ExternalInput")
    wq_pk = dt("wq_pk", [NH, 128, NDT, 128], F32, kind="ExternalInput")
    wk_pk = dt("wk_pk", [NKV, 128, NDT, 128], F32, kind="ExternalInput")
    wv_pk = dt("wv_pk", [2, 128, NDT, 512], F32, kind="ExternalInput")
    wg_pk = dt("wg_pk", [NJT, 128, NDT, 128], F32, kind="ExternalInput")
    wu_pk = dt("wu_pk", [NJT, 128, NDT, 128], F32, kind="ExternalInput")
    wd_pk = dt("wd_pk", [8, 4, 128, 8, 512], F32, kind="ExternalInput")
    wn_pk = dt("wn_pk", [8, 22, 128, 4, 512], F32, kind="ExternalInput")
    ga_pk = dt("ga_pk", [128, NDT, RANK], F32, kind="ExternalInput")
    ua_pk = dt("ua_pk", [128, NDT, RANK], F32, kind="ExternalInput")
    gbT = dt("gbT", [RANK, DF], F32, kind="ExternalInput")
    ubT = dt("ubT", [RANK, DF], F32, kind="ExternalInput")
    da_pk = dt("da_pk", [128, NJT, RANK], F32, kind="ExternalInput")
    dbT = dt("dbT", [RANK, D], F32, kind="ExternalInput")
    bq_pk = dt("bq_pk", [128, NH], F32, kind="ExternalInput")
    bk_pk = dt("bk_pk", [128, NKV], F32, kind="ExternalInput")
    bv_pk = dt("bv_pk", [1, 1024], F32, kind="ExternalInput")
    bd_pk = dt("bd_pk", [1, D], F32, kind="ExternalInput")
    mask_pk = dt("mask_pk", [128, 8 * 512], F32, kind="ExternalInput")
    cosk = dt("cosk", [128, SKV], F32, kind="ExternalInput")
    sink = dt("sink", [128, SKV], F32, kind="ExternalInput")
    coso = dt("coso", [128, ROWS], F32, kind="ExternalInput")
    sino = dt("sino", [128, ROWS], F32, kind="ExternalInput")
    biaspat = dt("biaspat", [128, 2 * LU], F32, kind="ExternalInput")
    ones_col = dt("ones_col", [128, 1], F32, kind="ExternalInput")
    ones_colb = dt("ones_colb", [128, 1], BF16, kind="ExternalInput")
    ones_rowi = dt("ones_row", [1, 512], F32, kind="ExternalInput")
    xn_hbm = dt("xn_hbm", [D, SKV], BF16, kind=sk)
    q_hbm = dt("q_hbm", [D, ROWS], BF16, kind=sk)
    h_hbm = dt("h_hbm", [NJT, 128, ROWS], BF16, kind=sk)
    if _DEBUG:
        kv_dbg = dt("kv_dbg", [128, 49152], BF16, kind="ExternalOutput")
        ctx_dbg = dt("ctx_dbg", [128, 32768], BF16, kind="ExternalOutput")
    out_T = dt("out_T", [D, ROWS], F32, kind="ExternalOutput")

    ctx = ExitStack()
    with ctx:
        sb = lambda n, s, d: ctx.enter_context(nc.sbuf_tensor(n, s, d))
        resA = sb("resA", [128, 32768], BF16)
        arena = sb("arena", [128, 49152], BF16)
        scr = sb("scr", [128, 5120], BF16)
        mr = sb("mr", [128, 4096], F32)
        pool = sb("pool", [128, 7296], BF16)
        bpat = sb("bpat", [128, 2 * LU], F32)
        bq_sb = sb("bq_sb", [128, NH], F32)
        bk_sb = sb("bk_sb", [128, NKV], F32)
        tuu = sb("tuu", [128, 1536], BF16)
        onec_f = sb("onec_f", [128, 1], F32)
        onec_b = sb("onec_b", [128, 1], BF16)
        da_sb = sb("da_sb", [128, NJT * RANK], BF16)

        pA = ctx.enter_context(nc.psum_tensor("pA", [128, 1024], F32))
        pB = ctx.enter_context(nc.psum_tensor("pB", [128, 1024], F32))
        pC = ctx.enter_context(nc.psum_tensor("pC", [128, 1024], F32))
        pD = ctx.enter_context(nc.psum_tensor("pD", [128, 1024], F32))

        scrf = scr[:, :].bitcast(F32)
        mrb = mr[:, :].bitcast(BF16)
        poolf = pool[:, :].bitcast(F32)
        arenaf = arena[:, :].bitcast(F32)

        r_rstd = poolf[0:1, 0:512]
        r_mrs = poolf[0:1, 512:1024]
        r_ones = poolf[0:1, 1024:1536]
        r_den = poolf[0:1, 1536:2048]
        r_bv = poolf[0:1, 2048:3072]
        r_ksq = poolf[0:1, 3072:3080]
        r_sc = poolf[0:1, 3080:3088]
        p_bmat = poolf[:, 3088:3128]
        r_bd = poolf[0:1, 3128:3640]
        p_bcs = poolf[:, 0:512]
        p_mball = poolf[:, 512:576]
        r_m = mr[0:1, 1024:1536]
        r_es = mr[0:1, 1536:2048]
        r_m2 = mr[0:1, 3072:3584]
        r_var = mr[0:1, 3584:4096]
        r_ln = mr[0:1, 1536:2048]
        cos_o = arenaf[:, 0:1024]
        sin_o = arenaf[:, 1024:2048]

        mk = lambda name: _Sem(ctx, nc, name)
        block = ctx.enter_context(nc.Block())
        E = {}

        C0 = mk("C0")

        @block.gpsimd
        def _(g):
            C0.inc(g.dma_start(bpat[:, :], biaspat[:, :]), 16)
            C0.inc(g.dma_start(bq_sb[:, :], bq_pk[:, :]), 16)
            C0.inc(g.dma_start(bk_sb[:, :], bk_pk[:, :]), 16)
            C0.inc(g.dma_start(onec_f[:, :], ones_col[:, :]), 16)
            C0.inc(g.dma_start(onec_b[:, :], ones_colb[:, :]), 16)
            C0.inc(g.dma_start(r_ones, ones_rowi[:, :]), 16)
            C0.inc(g.dma_start(r_bv, bv_pk[:, :]), 16)
            C0.inc(g.dma_start(da_sb[:, :], da_pk[:, :].rearrange("p a b -> p (a b)")), 16)
        NC0 = C0.n

        if _PHASES >= 1:
            NSB1 = 12
            L1 = mk("L1")
            SQ1 = mk("SQ1")
            TM1 = mk("TM1")
            VR1 = mk("VR1")
            SL1 = mk("SL1")
            VR2 = mk("VR2")
            BC1 = mk("BC1")
            VX1 = mk("VX1")
            XS1 = mk("XS1")

            def strip1(sbk):
                return arenaf[:, (sbk % 2) * 8192:(sbk % 2) * 8192 + 8192]

            def stage1(sbk):
                return arena[:, 32768 + (sbk % 2) * 8192:
                             32768 + (sbk % 2) * 8192 + 8192]

            @block.sync
            def _(sy):
                for sbk in range(NSB1):
                    if sbk >= 2:
                        sy.wait_ge(VX1.h, 64 * (sbk - 1))
                    src3 = xpk[:, sbk * 256:(sbk + 1) * 256].rearrange(
                        "(dt p) s -> p dt s", p=128)
                    dst3 = strip1(sbk)[:, :].rearrange("p (dt s) -> p dt s", s=256)
                    L1.inc(sy.dma_start(dst3, src3), 16)

            @block.scalar
            def _(sc):
                sc.wait_ge(C0.h, NC0)
                for sbk in range(NSB1):
                    sc.wait_ge(L1.h, 16 * (sbk + 1))
                    stp = strip1(sbk)
                    for dti in range(NDT):
                        idx = sbk * NDT + dti
                        if idx >= 2:
                            sc.wait_ge(TM1.h, idx - 1)
                        slot = mr[:, (dti % 2) * 256:(dti % 2) * 256 + 256]
                        SQ1.inc(sc.activation(
                            slot, stp[:, dti * 256:(dti + 1) * 256], AF.Square))
                    sc.wait_ge(VR1.h, 4 * sbk + 4)
                    SL1.inc(sc.activation(r_ln[0:1, 0:256], r_m2[0:1, 0:256], AF.Ln))
                    sc.wait_ge(SL1.h, 2 * sbk + 1)
                    SL1.inc(sc.activation(r_rstd[0:1, 0:256], r_ln[0:1, 0:256],
                                          AF.Exp, scale=-0.5))
                    sc.wait_ge(VX1.h, 64 * (sbk + 1))
                    own = sbk >= 8
                    for dti in range(NDT):
                        if own:
                            s0 = (sbk - 8) * 256
                            srcd = resA[:, dti * 1024 + s0:dti * 1024 + s0 + 256]
                        else:
                            srcd = stage1(sbk)[:, dti * 256:(dti + 1) * 256]
                        XS1.inc(sc.dma_start(
                            xn_hbm[dti * 128:(dti + 1) * 128,
                                   sbk * 256:(sbk + 1) * 256], srcd), 16)

            @block.tensor
            def _(te):
                te.wait_ge(C0.h, NC0)
                for sbk in range(NSB1):
                    te.wait_ge(L1.h, 16 * (sbk + 1))
                    if sbk >= 1:
                        te.wait_ge(VR1.h, 4 * sbk)
                    stp = strip1(sbk)
                    for dti in range(NDT):
                        idx = sbk * NDT + dti
                        te.wait_ge(SQ1.h, idx + 1)
                        slot = mr[:, (dti % 2) * 256:(dti % 2) * 256 + 256]
                        te.matmul(pA[0:1, 0:256], onec_f[:, :],
                                  stp[:, dti * 256:(dti + 1) * 256],
                                  start=(dti == 0), stop=(dti == NDT - 1))
                        m = te.matmul(pB[0:1, 0:256], onec_f[:, :], slot,
                                      start=(dti == 0), stop=(dti == NDT - 1))
                        TM1.inc(m, 1)
                    te.wait_ge(VR2.h, sbk + 1)
                    if sbk >= 1:
                        te.wait_ge(VX1.h, 64 * sbk)
                    te.matmul(pC[:, 0:256], r_ones[0:1, 0:128],
                              r_rstd[0:1, 0:256], start=True, stop=True)
                    m = te.matmul(pC[:, 512:768], r_ones[0:1, 0:128],
                                  r_mrs[0:1, 0:256], start=True, stop=True)
                    BC1.inc(m, 1)

            @block.vector
            def _(ve):
                ve.wait_ge(C0.h, NC0)
                for sbk in range(NSB1):
                    ve.wait_ge(TM1.h, 32 * (sbk + 1))
                    ve.tensor_scalar_mul(r_m[0:1, 0:256], pA[0:1, 0:256], 1.0 / D)
                    VR1.inc(ve.tensor_scalar_mul(r_es[0:1, 0:256],
                                                 pB[0:1, 0:256], 1.0 / D))
                    ve.wait_ge(VR1.h, 4 * sbk + 1)
                    VR1.inc(ve.tensor_mul(r_m2[0:1, 0:256], r_m[0:1, 0:256],
                                          r_m[0:1, 0:256]))
                    ve.wait_ge(VR1.h, 4 * sbk + 2)
                    VR1.inc(ve.tensor_sub(r_var[0:1, 0:256], r_es[0:1, 0:256],
                                          r_m2[0:1, 0:256]))
                    ve.wait_ge(VR1.h, 4 * sbk + 3)
                    VR1.inc(ve.tensor_scalar_add(r_m2[0:1, 0:256],
                                                 r_var[0:1, 0:256], 1e-5))
                    ve.wait_ge(SL1.h, 2 * sbk + 2)
                    VR2.inc(ve.tensor_mul(r_mrs[0:1, 0:256], r_m[0:1, 0:256],
                                          r_rstd[0:1, 0:256]))
                    ve.wait_ge(BC1.h, sbk + 1)
                    own = sbk >= 8
                    if sbk >= 2:
                        ve.wait_ge(XS1.h, 512 * (sbk - 1))
                    for dti in range(NDT):
                        tmp = mr[:, 2048 + (dti % 2) * 256:2048 + (dti % 2) * 256 + 256]
                        VX1.inc(ve.tensor_mul(
                            tmp, strip1(sbk)[:, dti * 256:(dti + 1) * 256],
                            pC[:, 0:256]))
                        ve.wait_ge(VX1.h, 64 * sbk + 2 * dti + 1)
                        if own:
                            dst = resA[:, dti * 1024 + (sbk - 8) * 256:
                                       dti * 1024 + (sbk - 8) * 256 + 256]
                        else:
                            dst = stage1(sbk)[:, dti * 256:(dti + 1) * 256]
                        VX1.inc(ve.tensor_sub(dst, tmp, pC[:, 512:768]))

            E["xn_tiles"] = (VX1, VX1.n)
            E["xn_hbm"] = (XS1, XS1.n)


        if _PHASES >= 2:
            LW3 = mk("LW3")
            CT3 = mk("CT3")
            T3 = mk("T3")
            SE3 = mk("SE3")
            VR3 = mk("VR3")
            QS3 = mk("QS3")

            @block.gpsimd
            def _(g):
                g.wait_ge(E["xn_hbm"][0].h, E["xn_hbm"][1])
                for et in range(NH):
                    if et >= 2:
                        g.wait_ge(T3.h, et - 1)
                    src = wq_pk[et].rearrange("p a b -> p (a b)")
                    sl = arena[:, 32768 + (et % 2) * 4096:
                               32768 + (et % 2) * 4096 + 4096]
                    LW3.inc(g.dma_start(sl, src), 16)

            @block.tensor
            def _(te):
                te.wait_ge(E["xn_tiles"][0].h, E["xn_tiles"][1])
                for et in range(NH):
                    te.wait_ge(LW3.h, 16 * (et + 1))
                    if et >= 2:
                        te.wait_ge(SE3.h, et - 1)
                    pq = pA if et % 2 == 0 else pB
                    wsl = arena[:, 32768 + (et % 2) * 4096:
                                32768 + (et % 2) * 4096 + 4096]
                    for dti in range(NDT):
                        for sh2 in range(2):
                            m = te.matmul(pq[:, sh2 * 512:(sh2 + 1) * 512],
                                          wsl[:, dti * 128:(dti + 1) * 128],
                                          resA[:, dti * 1024 + sh2 * 512:
                                               dti * 1024 + sh2 * 512 + 512],
                                          start=(dti == 0), stop=(dti == NDT - 1))
                    T3.inc(m, 1)

            @block.scalar
            def _(sc):
                CT3.inc(sc.dma_start(cos_o, coso[:, :]), 16)
                CT3.inc(sc.dma_start(sin_o, sino[:, :]), 16)
                for et in range(NH):
                    sc.wait_ge(T3.h, et + 1)
                    if et >= 2:
                        sc.wait_ge(VR3.h, 5 * (et - 1))
                    pq = pA if et % 2 == 0 else pB
                    qf = mr[:, (et % 2) * 1024:(et % 2) * 1024 + 1024]
                    SE3.inc(sc.activation(qf, pq[:, :], AF.Identity,
                                          bias=bq_sb[:, et:et + 1]))
                    sc.wait_ge(VR3.h, 5 * (et + 1))
                    QS3.inc(sc.dma_start(
                        q_hbm[et * 128:(et + 1) * 128, :], scr[:, 4096:5120]), 16)

            @block.vector
            def _(ve):
                ve.wait_ge(CT3.h, 32)
                for et in range(NH):
                    ve.wait_ge(SE3.h, et + 1)
                    qf = mr[:, (et % 2) * 1024:(et % 2) * 1024 + 1024]
                    rr = mr[:, 2048:3072]
                    aa = mr[:, 3072:4096]
                    VR3.inc(ve.tensor_copy(aa[0:64, :], qf[64:128, :]))
                    VR3.inc(ve.tensor_copy(aa[64:128, :], qf[0:64, :]))
                    ve.wait_ge(VR3.h, 5 * et + 2)
                    VR3.inc(ve.tensor_mul(rr, aa, sin_o))
                    VR3.inc(ve.tensor_mul(aa, qf, cos_o))
                    ve.wait_ge(VR3.h, 5 * et + 4)
                    if et >= 1:
                        ve.wait_ge(QS3.h, 16 * et)
                    qo = scr[:, 4096:5120]
                    VR3.inc(ve.tensor_add(qo, aa, rr))

            E["q_hbm"] = (QS3, QS3.n)
            E["ph3_v"] = (VR3, VR3.n)

        if _PHASES >= 3:
            L4 = mk("L4")
            T4 = mk("T4")
            T4t = mk("T4t")
            S4 = mk("S4")
            V4 = mk("V4")
            HS4 = mk("HS4")
            U4 = mk("U4")

            @block.gpsimd
            def _(g):
                g.wait_ge(E["ph3_v"][0].h, E["ph3_v"][1])
                g.wait_ge(E["q_hbm"][0].h, E["q_hbm"][1])
                L4.inc(g.dma_start(scr[:, 4096:4608],
                                   ga_pk[:, :].rearrange("p a b -> p (a b)")), 16)
                L4.inc(g.dma_start(scr[:, 4608:5120],
                                   ua_pk[:, :].rearrange("p a b -> p (a b)")), 16)
                for jt in range(NJT):
                    if jt >= 2:
                        g.wait_ge(T4.h, 2 + 2 * (jt - 1))
                    elif jt >= 1:
                        g.wait_ge(T4.h, 2)
                    else:
                        g.wait_ge(T4.h, 2)
                    wga = arena[:, 12288 + (jt % 2) * 4096:
                                12288 + (jt % 2) * 4096 + 4096]
                    wua = arena[:, 20480 + (jt % 2) * 4096:
                                20480 + (jt % 2) * 4096 + 4096]
                    L4.inc(g.dma_start(wga,
                                       wg_pk[jt].rearrange("p a b -> p (a b)")), 16)
                    L4.inc(g.dma_start(wua,
                                       wu_pk[jt].rearrange("p a b -> p (a b)")), 16)
                    L4.inc(g.dma_start(scr[32:48, 4096 + (jt % 2) * 128:
                                           4096 + (jt % 2) * 128 + 128],
                                       gbT[:, jt * 128:(jt + 1) * 128]), 16)
                    L4.inc(g.dma_start(scr[64:80, 4352 + (jt % 2) * 128:
                                           4352 + (jt % 2) * 128 + 128],
                                       ubT[:, jt * 128:(jt + 1) * 128]), 16)

            @block.tensor
            def _(te):
                te.wait_ge(L4.h, 32)
                for which, lo in ((0, 4096), (1, 4608)):
                    if which == 1:
                        te.wait_ge(S4.h, 1)
                    for dti in range(NDT):
                        for sh2 in range(2):
                            m = te.matmul(pD[0:16, sh2 * 512:(sh2 + 1) * 512],
                                          scr[:, lo + dti * 16:lo + dti * 16 + 16],
                                          resA[:, dti * 1024 + sh2 * 512:
                                               dti * 1024 + sh2 * 512 + 512],
                                          start=(dti == 0), stop=(dti == NDT - 1))
                    T4.inc(m, 1)
                te.wait_ge(U4.h, 32)
                for jt in range(NJT):
                    te.wait_ge(L4.h, 32 + 64 * (jt + 1))
                    te.wait_ge(S4.h, 2 + jt)
                    wga = arena[:, 12288 + (jt % 2) * 4096:
                                12288 + (jt % 2) * 4096 + 4096]
                    wua = arena[:, 20480 + (jt % 2) * 4096:
                                20480 + (jt % 2) * 4096 + 4096]
                    for dti in range(NDT):
                        for sh2 in range(2):
                            te.matmul(pA[:, sh2 * 512:(sh2 + 1) * 512],
                                      wga[:, dti * 128:(dti + 1) * 128],
                                      resA[:, dti * 1024 + sh2 * 512:
                                           dti * 1024 + sh2 * 512 + 512],
                                      start=(dti == 0), stop=False)
                    for sh2 in range(2):
                        m = te.matmul(pA[:, sh2 * 512:(sh2 + 1) * 512],
                                      scr[32:48, 4096 + (jt % 2) * 128:
                                          4096 + (jt % 2) * 128 + 128],
                                      tuu[32:48, sh2 * 512:(sh2 + 1) * 512],
                                      start=False, stop=True)
                    T4.inc(m, 1)
                    if jt >= 1:
                        te.wait_ge(V4.h, jt)
                        hb = scr[:, 2048 + ((jt - 1) % 2) * 1024:
                                 2048 + ((jt - 1) % 2) * 1024 + 1024]
                        for sh2 in range(2):
                            m = te.matmul(pC[0:16, sh2 * 512:(sh2 + 1) * 512],
                                          da_sb[:, (jt - 1) * 16:jt * 16],
                                          hb[:, sh2 * 512:(sh2 + 1) * 512],
                                          start=(jt == 1), stop=False)
                        T4t.inc(m, 1)
                    te.wait_ge(V4.h, jt)
                    for dti in range(NDT):
                        for sh2 in range(2):
                            te.matmul(pB[:, sh2 * 512:(sh2 + 1) * 512],
                                      wua[:, dti * 128:(dti + 1) * 128],
                                      resA[:, dti * 1024 + sh2 * 512:
                                           dti * 1024 + sh2 * 512 + 512],
                                      start=(dti == 0), stop=False)
                    for sh2 in range(2):
                        m = te.matmul(pB[:, sh2 * 512:(sh2 + 1) * 512],
                                      scr[64:80, 4352 + (jt % 2) * 128:
                                          4352 + (jt % 2) * 128 + 128],
                                      tuu[64:80, sh2 * 512:(sh2 + 1) * 512],
                                      start=False, stop=True)
                    T4.inc(m, 1)
                te.wait_ge(V4.h, NJT)
                hb = scr[:, 2048 + ((NJT - 1) % 2) * 1024:
                         2048 + ((NJT - 1) % 2) * 1024 + 1024]
                for sh2 in range(2):
                    m = te.matmul(pC[0:16, sh2 * 512:(sh2 + 1) * 512],
                                  da_sb[:, (NJT - 1) * 16:NJT * 16],
                                  hb[:, sh2 * 512:(sh2 + 1) * 512],
                                  start=False, stop=True)
                T4t.inc(m, 1)

            @block.scalar
            def _(sc):
                for which, dst in ((0, 32), (1, 64)):
                    sc.wait_ge(T4.h, which + 1)
                    if which == 1:
                        sc.wait_ge(U4.h, 16)
                    S4.inc(sc.activation(scr[0:16, 2048:3072], pD[0:16, :], AF.Copy))
                    sc.wait_ge(S4.h, which + 1)
                    U4.inc(sc.dma_start(tuu[dst:dst + 16, 0:1024],
                                        scr[0:16, 2048:3072]), 16)
                for jt in range(NJT):
                    sc.wait_ge(T4.h, 2 + 2 * jt + 1)
                    if jt >= 1:
                        sc.wait_ge(V4.h, jt)
                    S4.inc(sc.activation(scrf[:, 0:1024], pA[:, :], AF.Silu))
                    if jt >= 1:
                        hb = scr[:, 2048 + ((jt - 1) % 2) * 1024:
                                 2048 + ((jt - 1) % 2) * 1024 + 1024]
                        HS4.inc(sc.dma_start(h_hbm[jt - 1], hb), 16)
                sc.wait_ge(V4.h, NJT)
                hb = scr[:, 2048 + ((NJT - 1) % 2) * 1024:
                         2048 + ((NJT - 1) % 2) * 1024 + 1024]
                HS4.inc(sc.dma_start(h_hbm[NJT - 1], hb), 16)
                sc.wait_ge(T4t.h, NJT)
                S4.inc(sc.activation(tuu[0:16, 0:1024], pC[0:16, :], AF.Copy))

            @block.vector
            def _(ve):
                for jt in range(NJT):
                    ve.wait_ge(S4.h, 2 + jt + 1)
                    ve.wait_ge(T4.h, 2 + 2 * jt + 2)
                    if jt >= 2:
                        ve.wait_ge(HS4.h, 16 * (jt - 1))
                        ve.wait_ge(T4t.h, jt - 1)
                    hb = scr[:, 2048 + (jt % 2) * 1024:2048 + (jt % 2) * 1024 + 1024]
                    V4.inc(ve.tensor_mul(hb, scrf[:, 0:1024], pB[:, :]))

            E["h_hbm"] = (HS4, HS4.n)
            E["t_sb"] = (S4, S4.n)
            E["ph4_t"] = (T4, T4.n)
            E["ph4_v"] = (V4, V4.n)

        if _PHASES >= 4:
            L2 = mk("L2")
            LS2 = mk("LS2")
            KT2 = mk("KT2")
            T2 = mk("T2")
            S2 = mk("S2")
            V2 = mk("V2")
            BV2 = mk("BV2")
            KS2 = mk("KS2")
            passes = [("k", 0), ("k", 1), ("v", 0), ("v", 1)]

            @block.gpsimd
            def _(g):
                g.wait_ge(E["h_hbm"][0].h, E["h_hbm"][1])
                g.wait_ge(E["ph4_v"][0].h, E["ph4_v"][1])
                for el in range(4):
                    src = wk_pk[el].rearrange("p a b -> p (a b)")
                    L2.inc(g.dma_start(
                        resA[:, 16384 + el * 4096:16384 + (el + 1) * 4096],
                        src), 16)
                for el in range(4):
                    src = wk_pk[4 + el].rearrange("p a b -> p (a b)")
                    L2.inc(g.dma_start(
                        arena[:, 24576 + el * 4096:24576 + (el + 1) * 4096],
                        src), 16)
                g.wait_ge(T2.h, 24)
                src = wv_pk[0].rearrange("p a b -> p (a b)")
                L2.inc(g.dma_start(resA[:, 16384:32768], src), 16)
                g.wait_ge(T2.h, 72)
                src = wv_pk[1].rearrange("p a b -> p (a b)")
                L2.inc(g.dma_start(resA[:, 16384:32768], src), 16)

            @block.sync
            def _(sy):
                sy.wait_ge(E["xn_hbm"][0].h, E["xn_hbm"][1])
                sy.wait_ge(E["ph4_t"][0].h, E["ph4_t"][1])
                for pi in range(4):
                    for sbk in range(NSB):
                        idx = pi * NSB + sbk
                        if idx >= 1:
                            sy.wait_ge(T2.h, 4 * idx)
                        for q in range(4):
                            src3 = xn_hbm[1024 * q:1024 * (q + 1),
                                          sbk * 512:(sbk + 1) * 512].rearrange(
                                "(dt p) s -> p dt s", p=128)
                            dst3 = resA[:, q * 4096:(q + 1) * 4096].rearrange(
                                "p (dt s) -> p dt s", s=512)
                            LS2.inc(sy.dma_start(dst3, src3), 16)

            LW_CUM = [4, 8, 9, 10]

            @block.tensor
            def _(te):
                te.wait_ge(E["t_sb"][0].h, E["t_sb"][1])
                whome = {0: resA[:, 16384:32768], 1: arena[:, 24576:40960],
                         2: resA[:, 16384:32768], 3: resA[:, 16384:32768]}
                for pi, (kind, half) in enumerate(passes):
                    te.wait_ge(L2.h, 16 * LW_CUM[pi])
                    wh = whome[pi]
                    if kind == "v":
                        m = te.matmul(pD[:, 0:512], r_ones[0:1, 0:128],
                                      r_bv[0:1, half * 512:half * 512 + 512],
                                      start=True, stop=True)
                        BV2.inc(m, 1)
                    for sbk in range(NSB):
                        idx = pi * NSB + sbk
                        stp = resA[:, 0:16384]
                        if kind == "k":
                            for el in range(4):
                                tix = idx * 4 + el
                                if tix >= 4:
                                    te.wait_ge(V2.h, 6 * (tix - 4) + 1)
                                ps = (pA, pB)[el // 2][:, (el % 2) * 512:
                                                       (el % 2) * 512 + 512]
                                for dti in range(NDT):
                                    if el == 0 and dti % 8 == 0:
                                        te.wait_ge(LS2.h,
                                                   16 * (4 * idx + dti // 8 + 1))
                                    m = te.matmul(
                                        ps, wh[:, (el * NDT + dti) * 128:
                                               (el * NDT + dti) * 128 + 128],
                                        stp[:, dti * 512:(dti + 1) * 512],
                                        start=(dti == 0), stop=(dti == NDT - 1))
                                T2.inc(m, 1)
                        else:
                            for st in range(4):
                                tix = ((pi - 2) * NSB + sbk) * 4 + st
                                if tix >= 2:
                                    te.wait_ge(V2.h, 288 + tix - 1)
                                ps = pC[:, (st % 2) * 512:(st % 2) * 512 + 512]
                                for dti in range(NDT):
                                    if st == 0 and dti % 8 == 0:
                                        te.wait_ge(LS2.h,
                                                   16 * (4 * idx + dti // 8 + 1))
                                    m = te.matmul(
                                        ps, stp[:, dti * 512 + st * 128:
                                                dti * 512 + st * 128 + 128],
                                        wh[:, dti * 512:(dti + 1) * 512],
                                        start=(dti == 0), stop=(dti == NDT - 1))
                                T2.inc(m, 1)
                for g8 in range(NKV):
                    for sbk in range(NSB):
                        i2 = g8 * NSB + sbk
                        te.wait_ge(S2.h, 2 + i2 + 1)
                        if i2 >= 1:
                            te.wait_ge(KS2.h, 3 * i2 - 1)
                        m = te.matmul(pD[0:1, 512:1024], onec_f[:, :],
                                      mr[:, 0:512], start=True, stop=True)
                        KS2.inc(m, 1)

            @block.scalar
            def _(sc):
                for pi in range(2):
                    for sbk in range(NSB):
                        kidx = pi * NSB + sbk
                        if kidx >= 2:
                            sc.wait_ge(V2.h, 24 * (kidx - 1))
                        co = mr[:, 1024 + (sbk % 2) * 512:1024 + (sbk % 2) * 512 + 512]
                        si = mr[:, 2048 + (sbk % 2) * 512:2048 + (sbk % 2) * 512 + 512]
                        KT2.inc(sc.dma_start(co, cosk[:, sbk * 512:(sbk + 1) * 512]), 16)
                        KT2.inc(sc.dma_start(si, sink[:, sbk * 512:(sbk + 1) * 512]), 16)
                for half in range(2):
                    sc.wait_ge(BV2.h, half + 1)
                    sc.wait_ge(V2.h, 288 if half == 0 else 288 + 24)
                    S2.inc(sc.activation(mr[:, 3072:3584], pD[:, 0:512], AF.Copy))
                for g8 in range(NKV):
                    for sbk in range(NSB):
                        i2 = g8 * NSB + sbk
                        if i2 >= 1:
                            sc.wait_ge(KS2.h, 3 * i2 - 2)
                        S2.inc(sc.activation(
                            mr[:, 0:512],
                            arena[:, g8 * 3072 + sbk * 512:g8 * 3072 + sbk * 512 + 512],
                            AF.Square))

            @block.vector
            def _(ve):
                for pi in range(2):
                    for sbk in range(NSB):
                        kidx = pi * NSB + sbk
                        for el in range(4):
                            tix = kidx * 4 + el
                            ve.wait_ge(T2.h, tix + 1)
                            ve.wait_ge(KT2.h, 32 * (kidx + 1))
                            g8 = 4 * pi + el
                            ps = (pA, pB)[el // 2][:, (el % 2) * 512:
                                                   (el % 2) * 512 + 512]
                            kev = mr[:, (tix % 2) * 512:(tix % 2) * 512 + 512]
                            co = mr[:, 1024 + (sbk % 2) * 512:1024 + (sbk % 2) * 512 + 512]
                            si = mr[:, 2048 + (sbk % 2) * 512:2048 + (sbk % 2) * 512 + 512]
                            rr = mr[:, 3072:3584]
                            aa = mr[:, 3584:4096]
                            V2.inc(ve.tensor_scalar_add(kev, ps, bk_sb[:, g8:g8 + 1]))
                            ve.wait_ge(V2.h, 6 * tix + 1)
                            V2.inc(ve.tensor_copy(aa[0:64, :], kev[64:128, :]))
                            V2.inc(ve.tensor_copy(aa[64:128, :], kev[0:64, :]))
                            ve.wait_ge(V2.h, 6 * tix + 3)
                            V2.inc(ve.tensor_mul(rr, aa, si))
                            V2.inc(ve.tensor_mul(aa, kev, co))
                            ve.wait_ge(V2.h, 6 * tix + 5)
                            dst = arena[:, g8 * 3072 + sbk * 512:
                                        g8 * 3072 + sbk * 512 + 512]
                            V2.inc(ve.tensor_add(dst, aa, rr))
                for pi in range(2):
                    ve.wait_ge(S2.h, pi + 1)
                    for sbk in range(NSB):
                        for st in range(4):
                            tix = (pi * NSB + sbk) * 4 + st
                            ve.wait_ge(T2.h, 48 + tix + 1)
                            ps = pC[:, (st % 2) * 512:(st % 2) * 512 + 512]
                            vt = sbk * 4 + st
                            dst = arena[:, 24576 + vt * 1024 + pi * 512:
                                        24576 + vt * 1024 + pi * 512 + 512]
                            V2.inc(ve.tensor_add(dst, ps, mr[:, 3072:3584]))
                for g8 in range(NKV):
                    for sbk in range(NSB):
                        i2 = g8 * NSB + sbk
                        ve.wait_ge(KS2.h, 3 * i2 + 1)
                        KS2.inc(ve.reduce_max(r_sc[0:1, 0:1], pD[0:1, 512:1024],
                                              axis=mybir.AxisListType.X))
                        ve.wait_ge(KS2.h, 3 * i2 + 2)
                        if sbk == 0:
                            KS2.inc(ve.tensor_copy(r_ksq[0:1, g8:g8 + 1],
                                                   r_sc[0:1, 0:1]))
                        else:
                            KS2.inc(ve.tensor_max(r_ksq[0:1, g8:g8 + 1],
                                                  r_ksq[0:1, g8:g8 + 1],
                                                  r_sc[0:1, 0:1]))

            E["ph2_t"] = (T2, T2.n)
            E["ph2_s"] = (S2, S2.n)
            E["ph2_v"] = (V2, V2.n)
            E["ph2_ks"] = (KS2, KS2.n)

        if _PHASES >= 5:
            M5 = mk("M5")
            Q5 = mk("Q5")
            S5p = mk("S5p")
            T5p = mk("T5p")
            V5p = mk("V5p")
            S5q = mk("S5q")
            V5b = mk("V5b")
            T5s = mk("T5s")
            V5e = mk("V5e")
            S5e = mk("S5e")
            T5c = mk("T5c")
            V5r = mk("V5r")
            T5d = mk("T5d")
            S5d = mk("S5d")
            V5c = mk("V5c")

            e_in = (scrf[:, 1536:2048], scrf[:, 2048:2560])
            sqs = scrf[:, 1536:2048]
            iters = [(qt, g8, i) for qt in range(2) for g8 in range(NKV)
                     for i in range(4)]
            NIT = len(iters)

            def esl4(uix):
                return scr[:, 1024 + (uix % 4) * 512:1024 + (uix % 4) * 512 + 512]

            def psc3(uix):
                return (pA[:, 0:512], pA[:, 512:1024],
                        pB[:, 512:1024], pD[:, 0:512])[uix % 4]

            def qslot(it2):
                return scr[:, (it2 % 2) * 512:(it2 % 2) * 512 + 512]

            @block.gpsimd
            def _(g):
                g.wait_ge(E["ph2_v"][0].h, E["ph2_v"][1])
                g.wait_ge(E["ph2_ks"][0].h, E["ph2_ks"][1])
                M5.inc(g.dma_start(mr[:, :], mask_pk[:, :]), 16)

            @block.scalar
            def _(sc):
                sc.wait_ge(E["q_hbm"][0].h, E["q_hbm"][1])
                sc.wait_ge(E["ph2_t"][0].h, E["ph2_t"][1])
                sc.wait_ge(E["ph2_ks"][0].h, E["ph2_ks"][1])
                for it, (qt, g8, i) in enumerate(iters):
                    h = 4 * g8 + i
                    if it >= 2:
                        sc.wait_ge(S5p.h, it - 1)
                    Q5.inc(sc.dma_start(
                        qslot(it), q_hbm[h * 128:(h + 1) * 128,
                                         qt * 512:qt * 512 + 512]), 16)
                    sc.wait_ge(Q5.h, 16 * (it + 1))
                    if it >= 1:
                        sc.wait_ge(T5p.h, 2 * it - 1)
                    S5p.inc(sc.activation(sqs, qslot(it), AF.Square))
                    if it >= 1:
                        sc.wait_ge(T5p.h, 2 * it)
                        S5q.inc(sc.activation(p_mball[:, it - 1:it],
                                              pD[:, 0:1], AF.Copy,
                                              scale=1.0 / 256))
                sc.wait_ge(T5p.h, 2 * NIT)
                S5q.inc(sc.activation(p_mball[:, NIT - 1:NIT],
                                      pD[:, 0:1], AF.Copy, scale=1.0 / 256))

            @block.tensor
            def _(te):
                te.wait_ge(E["ph2_s"][0].h, E["ph2_s"][1])
                te.wait_ge(E["ph2_ks"][0].h, E["ph2_ks"][1])
                te.wait_ge(E["ph2_v"][0].h, E["ph2_v"][1])
                for it in range(NIT):
                    te.wait_ge(S5p.h, it + 1)
                    if it >= 1:
                        te.wait_ge(V5p.h, 2 * it)
                    m = te.matmul(pC[0:1, 512:1024], onec_f[:, :], sqs,
                                  start=True, stop=True)
                    T5p.inc(m, 1)
                    te.wait_ge(V5p.h, 2 * (it + 1))
                    if it >= 1:
                        te.wait_ge(S5q.h, it)
                    m = te.matmul(pD[:, 0:1], r_ones[0:1, 0:128],
                                  r_sc[0:1, 1:2], start=True, stop=True)
                    T5p.inc(m, 1)

            @block.vector
            def _(ve):
                ve.wait_ge(M5.h, 16)
                for it, (qt, g8, i) in enumerate(iters):
                    ve.wait_ge(T5p.h, 2 * it + 1)
                    V5p.inc(ve.reduce_max(r_sc[0:1, 0:1], pC[0:1, 512:1024],
                                          axis=mybir.AxisListType.X))
                    ve.wait_ge(V5p.h, 2 * it + 1)
                    if it >= 1:
                        ve.wait_ge(T5p.h, 2 * it)
                    V5p.inc(ve.tensor_add(r_sc[0:1, 1:2], r_sc[0:1, 0:1],
                                          r_ksq[0:1, g8:g8 + 1]))

            @block.scalar
            def _(sc):
                def qmload(it2):
                    if it2 >= 2:
                        sc.wait_ge(T5s.h, LU * (it2 - 1))
                    h2 = 4 * iters[it2][1] + iters[it2][2]
                    Q5.inc(sc.dma_start(
                        qslot(it2), q_hbm[h2 * 128:(h2 + 1) * 128,
                                          iters[it2][0] * 512:
                                          iters[it2][0] * 512 + 512]), 16)

                qmload(0)
                qmload(1)
                for it, (qt, g8, i) in enumerate(iters):
                    h = 4 * g8 + i
                    for u in range(LU):
                        uix = it * LU + u
                        sc.wait_ge(T5s.h, uix + 1)
                        sc.wait_ge(V5b.h, it + 1)
                        if uix >= 4:
                            sc.wait_ge(T5c.h, 2 * (uix - 3))
                        if u in MASK_SLOT:
                            sc.wait_ge(V5e.h, 8 * it + MASK_SLOT[u] + 1)
                            srcap = e_in[u % 2]
                        else:
                            srcap = psc3(uix)
                        S5e.inc(sc.activation(esl4(uix), srcap, AF.Exp, scale=SM,
                                              bias=p_bmat[:, (it % 2) * 20 + u:
                                                          (it % 2) * 20 + u + 1]))
                    sc.wait_ge(T5d.h, it + 1)
                    if it >= 1:
                        sc.wait_ge(V5c.h, it)
                    S5d.inc(sc.activation(p_bcs, pD[:, 512:1024], AF.Copy))
                    if it + 2 < NIT:
                        qmload(it + 2)

            @block.tensor
            def _(te):
                te.wait_ge(S5q.h, NIT)
                for it, (qt, g8, i) in enumerate(iters):
                    qsl = qslot(it)
                    te.wait_ge(Q5.h, 16 * (NIT + it + 1))
                    for u in range(LU + 2):
                        if u < LU:
                            uix = it * LU + u
                            kt = _unit_kt(qt, u)
                            if uix >= 4:
                                pix = uix - 4
                                te.wait_ge(S5e.h, pix + 1)
                                pu = pix % LU
                                if pu in MASK_SLOT:
                                    te.wait_ge(V5e.h, 8 * (pix // LU) +
                                               MASK_SLOT[pu] + 1)
                            m = te.matmul(psc3(uix),
                                          arena[:, g8 * 3072 + kt * 128:
                                                g8 * 3072 + kt * 128 + 128],
                                          qsl, start=True, stop=True)
                            T5s.inc(m, 1)
                        if u >= 2:
                            v = u - 2
                            vix = it * LU + v
                            ktv = _unit_kt(qt, v)
                            te.wait_ge(S5e.h, vix + 1)
                            if v == 0:
                                te.wait_ge(V5c.h, it)
                                te.wait_ge(V5r.h, it)
                            te.matmul(pB[:, 0:512],
                                      arena[:, 24576 + ktv * 1024 + g8 * 128:
                                            24576 + ktv * 1024 + g8 * 128 + 128],
                                      esl4(vix), start=(v == 0), stop=(v == LU - 1))
                            m = te.matmul(pC[0:1, 0:512], onec_b[:, :], esl4(vix),
                                          start=(v == 0), stop=(v == LU - 1))
                            T5c.inc(m, 2)
                    te.wait_ge(V5r.h, it + 1)
                    m = te.matmul(pD[:, 512:1024], r_ones[0:1, 0:128], r_den,
                                  start=True, stop=True)
                    T5d.inc(m, 1)

            @block.vector
            def _(ve):
                for it, (qt, g8, i) in enumerate(iters):
                    ve.wait_ge(S5q.h, it + 1)
                    if it >= 2:
                        ve.wait_ge(S5e.h, LU * (it - 1))
                    V5b.inc(ve.tensor_scalar(
                        p_bmat[:, (it % 2) * 20:(it % 2) * 20 + 20],
                        bpat[:, qt * LU:qt * LU + LU],
                        p_mball[:, it:it + 1], None, ALU.subtract))
                    ve.wait_ge(S5e.h, LU * it)
                    prev_same_slot = {10: 8, 11: 9, 16: 10, 17: 11, 18: 16, 19: 17}
                    for u in MASK_UNITS:
                        uix = it * LU + u
                        ve.wait_ge(T5s.h, uix + 1)
                        if u in prev_same_slot:
                            ve.wait_ge(S5e.h, it * LU + prev_same_slot[u] + 1)
                        slot = MASK_SLOT[u]
                        V5e.inc(ve.tensor_add(
                            e_in[u % 2], psc3(uix),
                            mr[:, slot * 512:(slot + 1) * 512]))
                    ve.wait_ge(T5c.h, 2 * LU * (it + 1))
                    ve.wait_ge(T5d.h, it)
                    V5r.inc(ve.reciprocal(r_den, pC[0:1, 0:512]))
                    ve.wait_ge(S5d.h, it + 1)
                    h = 4 * g8 + i
                    dst = resA[:, h * 1024 + qt * 512:h * 1024 + qt * 512 + 512]
                    V5c.inc(ve.tensor_mul(dst, pB[:, 0:512], p_bcs))

            E["ctx"] = (V5c, V5c.n)
            E["ph5_se"] = (S5e, S5e.n)
            E["ph5_vr"] = (V5r, V5r.n)
            E["ph5_sd"] = (S5d, S5d.n)
            E["ph5_tc"] = (T5c, T5c.n)

        if _PHASES >= 6:
            L6 = mk("L6")
            H6 = mk("H6")
            X6 = mk("X6")
            B6 = mk("B6")
            T6d = mk("T6d")
            T6n = mk("T6n")
            V6 = mk("V6")
            OUT = mk("OUT")
            psl = (pA, pB, pC, pD)

            @block.gpsimd
            def _(g):
                g.wait_ge(E["ph5_se"][0].h, E["ph5_se"][1])
                g.wait_ge(E["ctx"][0].h, E["ctx"][1])
                g.wait_ge(E["ph5_tc"][0].h, E["ph5_tc"][1])
                for og in range(8):
                    if og >= 1:
                        g.wait_ge(T6d.h, 4 * og)
                    L6.inc(g.dma_start(tuu[0:16, 1024:1536],
                                       dbT[:, og * 512:og * 512 + 512]), 16)
                    for dtg in range(4):
                        dix = og * 4 + dtg
                        if dix >= 2:
                            g.wait_ge(T6d.h, dix - 1)
                        src = wd_pk[og, dtg].rearrange("p a b -> p (a b)")
                        sl = arena[:, (dix % 2) * 4096:(dix % 2) * 4096 + 4096]
                        L6.inc(g.dma_start(sl, src), 16)
                    for cc in range(22):
                        cix = og * 22 + cc
                        if cix >= 2:
                            pog, pcc = divmod(cix - 2, 22)
                            g.wait_ge(T6n.h, pog * 43 + min(43, 2 * pcc + 2))
                        src = wn_pk[og, cc].rearrange("p a b -> p (a b)")
                        sl = arena[:, 8192 + (cix % 2) * 2048:
                                   8192 + (cix % 2) * 2048 + 2048]
                        L6.inc(g.dma_start(sl, src), 16)

            @block.sync
            def _(sy):
                sy.wait_ge(E["h_hbm"][0].h, E["h_hbm"][1])
                sy.wait_ge(E["ph5_se"][0].h, E["ph5_se"][1])
                sy.wait_ge(E["ph5_sd"][0].h, E["ph5_sd"][1])
                for og in range(8):
                    if og >= 1:
                        sy.wait_ge(T6d.h, 4 * og)
                    B6.inc(sy.dma_start(r_bd[0:1, 0:512],
                                        bd_pk[:, og * 512:og * 512 + 512]), 16)
                    for hc in range(43):
                        if hc >= 2:
                            sy.wait_ge(T6n.h, og * 43 + hc - 1)
                        elif og > 0:
                            sy.wait_ge(T6n.h, og * 43)
                        hb = mrb[:, (hc % 2) * 2048:(hc % 2) * 2048 + 2048]
                        src = h_hbm[2 * hc:2 * hc + 2].rearrange("a p s -> p a s")
                        dst = hb.rearrange("p (a s) -> p a s", s=1024)
                        H6.inc(sy.dma_start(dst, src), 16)
                    for ot in range(4):
                        oix = og * 4 + ot
                        if oix >= 1:
                            sy.wait_ge(V6.h, 2 * oix)
                        X6.inc(sy.dma_start(
                            mr[:, 2048:3072],
                            xpk[(og * 4 + ot) * 128:(og * 4 + ot) * 128 + 128,
                                2048:3072]), 16)

            @block.tensor
            def _(te):
                te.wait_ge(E["ctx"][0].h, E["ctx"][1])
                te.wait_ge(E["ph5_vr"][0].h, E["ph5_vr"][1])
                te.wait_ge(E["ph5_sd"][0].h, E["ph5_sd"][1])
                te.wait_ge(E["ph5_se"][0].h, E["ph5_se"][1])
                for og in range(8):
                    te.wait_ge(B6.h, 16 * (og + 1))
                    te.wait_ge(L6.h, 16 * (og * 27 + 1))
                    if og >= 1:
                        te.wait_ge(V6.h, 8 * og)
                    for ot in range(4):
                        for sh2 in range(2):
                            te.matmul(psl[ot][:, sh2 * 512:(sh2 + 1) * 512],
                                      r_bd[0:1, ot * 128:ot * 128 + 128],
                                      r_ones, start=True, stop=False)
                            te.matmul(psl[ot][:, sh2 * 512:(sh2 + 1) * 512],
                                      tuu[0:16, 1024 + ot * 128:1024 + ot * 128 + 128],
                                      tuu[0:16, sh2 * 512:(sh2 + 1) * 512],
                                      start=False, stop=False)
                    for dtg in range(4):
                        dix = og * 4 + dtg
                        te.wait_ge(L6.h, 16 * (og * 27 + 2 + dtg))
                        wsl = arena[:, (dix % 2) * 4096:(dix % 2) * 4096 + 4096]
                        for ot in range(4):
                            for di in range(8):
                                dti = dtg * 8 + di
                                for sh2 in range(2):
                                    m = te.matmul(
                                        psl[ot][:, sh2 * 512:(sh2 + 1) * 512],
                                        wsl[:, (di * 4 + ot) * 128:
                                            (di * 4 + ot) * 128 + 128],
                                        resA[:, dti * 1024 + sh2 * 512:
                                             dti * 1024 + sh2 * 512 + 512],
                                        start=False, stop=False)
                        T6d.inc(m, 1)
                    for hc in range(43):
                        cc = hc // 2
                        cix = og * 22 + cc
                        te.wait_ge(L6.h, 16 * (og * 27 + 6 + cc))
                        te.wait_ge(H6.h, 16 * (og * 43 + hc + 1))
                        wsl = arena[:, 8192 + (cix % 2) * 2048:
                                    8192 + (cix % 2) * 2048 + 2048]
                        hbp = mrb[:, (hc % 2) * 2048:(hc % 2) * 2048 + 2048]
                        last_hc = (hc == 42)
                        for ot in range(4):
                            for jin in range(2):
                                jt = 2 * hc + jin
                                jt_in = jt - 4 * cc
                                hb = hbp[:, jin * 1024:jin * 1024 + 1024]
                                for sh2 in range(2):
                                    m = te.matmul(
                                        psl[ot][:, sh2 * 512:(sh2 + 1) * 512],
                                        wsl[:, (jt_in * 4 + ot) * 128:
                                            (jt_in * 4 + ot) * 128 + 128],
                                        hb[:, sh2 * 512:(sh2 + 1) * 512],
                                        start=False, stop=(last_hc and jin == 1))
                        T6n.inc(m, 1)

            @block.vector
            def _(ve):
                for og in range(8):
                    ve.wait_ge(T6n.h, 43 * (og + 1))
                    for ot in range(4):
                        oix = og * 4 + ot
                        ve.wait_ge(X6.h, 16 * (oix + 1))
                        for sh2 in range(2):
                            oix2 = 2 * oix + sh2
                            if oix2 >= 2:
                                ve.wait_ge(OUT.h, 16 * (oix2 - 1))
                            osl = mr[:, 3072 + (oix2 % 2) * 512:
                                     3072 + (oix2 % 2) * 512 + 512]
                            V6.inc(ve.tensor_add(
                                osl, psl[ot][:, sh2 * 512:(sh2 + 1) * 512],
                                mr[:, 2048 + sh2 * 512:2048 + sh2 * 512 + 512]))

            @block.scalar
            def _(sc):
                for og in range(8):
                    for ot in range(4):
                        for sh2 in range(2):
                            oix2 = (og * 4 + ot) * 2 + sh2
                            sc.wait_ge(V6.h, oix2 + 1)
                            osl = mr[:, 3072 + (oix2 % 2) * 512:
                                     3072 + (oix2 % 2) * 512 + 512]
                            OUT.inc(sc.dma_start(
                                out_T[(og * 4 + ot) * 128:
                                      (og * 4 + ot) * 128 + 128,
                                      sh2 * 512:(sh2 + 1) * 512], osl), 16)
                sc.wait_ge(OUT.h, 1024)

        if _DEBUG and _PHASES >= 4:
            DBG = mk("DBG")

            @block.sync
            def _(sy):
                sy.wait_ge(E["ph2_v"][0].h, E["ph2_v"][1])
                DBG.inc(sy.dma_start(kv_dbg[:, :], arena[:, :]), 16)
                if _PHASES >= 5:
                    sy.wait_ge(E["ctx"][0].h, E["ctx"][1])
                    DBG.inc(sy.dma_start(ctx_dbg[:, :], resA[:, :]), 16)
                sy.wait_ge(DBG.h, 16 * (2 if _PHASES >= 5 else 1))

        if _PHASES < 6:
            FW = mk("FW")

            @block.sync
            def _(sy):
                if "ctx" in E:
                    sy.wait_ge(E["ctx"][0].h, E["ctx"][1])
                elif "ph2_v" in E:
                    sy.wait_ge(E["ph2_v"][0].h, E["ph2_v"][1])
                elif "h_hbm" in E:
                    sy.wait_ge(E["h_hbm"][0].h, E["h_hbm"][1])
                elif "q_hbm" in E:
                    sy.wait_ge(E["q_hbm"][0].h, E["q_hbm"][1])
                elif "xn_hbm" in E:
                    sy.wait_ge(E["xn_hbm"][0].h, E["xn_hbm"][1])
                FW.inc(sy.dma_start(out_T[0:128, 0:512], mr[:, 0:512]), 16)
                sy.wait_ge(FW.h, 16)

    return nc


def kernel(**inputs):
    in_maps = _pack_inputs(inputs)
    nc = bass.Bass("TRN2", target_bir_lowering=False)
    _build(nc)
    res = run_bass_kernel_spmd(nc, in_maps, core_ids=list(range(NCORES)))
    _LAST["res"] = res
    outs = [np.asarray(r["out_T"], dtype=np.float32) for r in res.results]
    full = np.concatenate(outs, axis=1)
    return np.ascontiguousarray(full.T).reshape(1, SEQ, D).astype(np.float32)


# version 11 (final)
# speedup vs baseline: 1.0809x; 1.0809x over previous
import numpy as np
import ml_dtypes
import concourse.bass as bass
import concourse.mybir as mybir
from concourse.bass_utils import run_bass_kernel_spmd
from contextlib import ExitStack

F32 = mybir.dt.float32
BF16 = mybir.dt.bfloat16
AF = mybir.ActivationFunctionType
ALU = mybir.AluOpType

_DEBUG = 0
_PHASES = 6
_LAST = {}

SEQ, D, NCORES = 8192, 4096, 8
ROWS = SEQ // NCORES
HD, NH, NKV = 128, 32, 8
DF, NJT, RANK = 11008, 86, 16
BLOCK, LOCAL, VERT = 64, 16, 8
NVSEG, NSEG2 = 16, 32
SKV = (NVSEG + NSEG2) * BLOCK
NEG = -1e30
SM = 1.0 / HD
NDT = D // 128
LU = 20
NSB = SKV // 512
MASK_UNITS = [8, 9, 10, 11, 16, 17, 18, 19]
MASK_SLOT = {u: s for s, u in enumerate(MASK_UNITS)}


def _verts_real(c):
    vbase = max(0, 16 * c - 15)
    verts = [j for j in range(vbase) if (j + 1) % VERT == 0]
    if c > 0:
        stray = [j for j in range(vbase, vbase + 8) if (j + 1) % VERT == 0]
        assert len(stray) == 1
        verts += stray
    return verts, vbase


def _packed_blocks(c):
    verts, vbase = _verts_real(c)
    vseg = verts + [-1] * (NVSEG - len(verts))
    seg2 = [b if b >= 0 else -1 for b in range(16 * c - 16, 16 * c + 16)]
    return vseg + seg2, len(verts), vbase


def _unit_kt(qt, u):
    return u if u < 8 else 8 + (u - 8) + 4 * qt


def _bias_pattern(c):
    packed, _, vbase = _packed_blocks(c)
    verts = packed[:NVSEG]
    pat = np.zeros((128, 2 * LU), dtype=np.float32)
    for qt in range(2):
        i0 = 16 * c + 8 * qt
        span_start = max(vbase, i0 - 15)
        span_end = i0 + 8
        nv_t = sum(1 for j in verts if 0 <= j < span_start)
        for u in range(LU):
            kt = _unit_kt(qt, u)
            col = np.full(128, NEG, dtype=np.float32)
            for half in range(2):
                pos = 2 * kt + half
                kb = packed[pos]
                if kb < 0:
                    continue
                ok = (pos < nv_t) if u < 8 else (span_start <= kb < span_end)
                if ok:
                    col[half * 64:(half + 1) * 64] = 0.0
            pat[:, qt * LU + u] = col
    return pat


def _mask_consts():
    m = np.zeros((128, 8 * 512), dtype=np.float32)
    for s in range(4):
        t = np.full((128, 512), NEG, dtype=np.float32)
        for half in range(2):
            j = 2 * s + half - 1
            if j < 0:
                continue
            for qb in range(8):
                if qb <= j or j % 8 == 6:
                    t[half * 64:(half + 1) * 64, qb * 64:(qb + 1) * 64] = 0.0
        m[:, s * 512:(s + 1) * 512] = t
    for dm in range(4):
        t = np.full((128, 512), NEG, dtype=np.float32)
        for half in range(2):
            kbrel = 2 * dm + half
            for qb in range(8):
                sub = t[half * 64:(half + 1) * 64, qb * 64:(qb + 1) * 64]
                if qb > kbrel:
                    sub[:] = 0.0
                elif qb == kbrel:
                    r = np.arange(64)
                    sub[:] = np.where(r[None, :] >= r[:, None], 0.0, NEG)
        m[:, (4 + dm) * 512:(5 + dm) * 512] = t
    return m


def _rope_tables(positions):
    inv = (np.float32(1.0) / (np.float32(1e6) **
           (np.arange(0, HD, 2, dtype=np.float32) / np.float32(HD)))).astype(np.float32)
    t = np.asarray(positions, dtype=np.float32)
    fr = np.outer(inv, t).astype(np.float32)
    return (np.concatenate([np.cos(fr), np.cos(fr)], 0).astype(np.float32),
            np.concatenate([-np.sin(fr), np.sin(fr)], 0).astype(np.float32))


def _pack_lhsT(WT, n_kt, n_mt):
    a = WT.reshape(n_kt, 128, n_mt, 128)
    return np.ascontiguousarray(np.transpose(a, (2, 1, 0, 3)))


def _pack_inputs(inputs):
    x = np.asarray(inputs["L0_attn_hidden_states"])[0]
    xT = np.ascontiguousarray(x.T)
    qrows = np.concatenate([np.arange(g * 768, g * 768 + 512) for g in range(NKV)])
    krows = np.concatenate([np.arange(g * 768 + 512, g * 768 + 640) for g in range(NKV)])
    vrows = np.concatenate([np.arange(g * 768 + 640, g * 768 + 768) for g in range(NKV)])
    w_qkv = np.asarray(inputs["L0_attn_w_qkv"])
    b_qkv = np.asarray(inputs["L0_attn_b_qkv"])
    sh = {}
    sh["wq_pk"] = _pack_lhsT(np.ascontiguousarray(w_qkv[qrows].T), NDT, NH)
    sh["wk_pk"] = _pack_lhsT(np.ascontiguousarray(w_qkv[krows].T), NDT, NKV)
    a = np.ascontiguousarray(w_qkv[vrows].T).reshape(NDT, 128, 2, 512)
    sh["wv_pk"] = np.ascontiguousarray(np.transpose(a, (2, 1, 0, 3)))
    sh["wg_pk"] = _pack_lhsT(np.ascontiguousarray(np.asarray(inputs["L0_ffn_gate_w"]).T), NDT, NJT)
    sh["wu_pk"] = _pack_lhsT(np.ascontiguousarray(np.asarray(inputs["L0_ffn_up_w"]).T), NDT, NJT)
    a = np.ascontiguousarray(np.asarray(inputs["L0_attn_w_dense"]).T).reshape(NDT, 128, 8, 4, 128)
    a = np.transpose(a, (2, 0, 1, 3, 4)).reshape(8, 4, 8, 128, 512)
    sh["wd_pk"] = np.ascontiguousarray(np.transpose(a, (0, 1, 3, 2, 4)))
    a = np.ascontiguousarray(np.asarray(inputs["L0_ffn_down_w"]).T).reshape(NJT, 128, 8, 4, 128)
    a = np.transpose(a, (2, 0, 1, 3, 4)).reshape(8, NJT, 128, 512)
    pad = np.zeros((8, 88 - NJT, 128, 512), dtype=a.dtype)
    a = np.concatenate([a, pad], axis=1).reshape(8, 22, 4, 128, 512)
    sh["wn_pk"] = np.ascontiguousarray(np.transpose(a, (0, 1, 3, 2, 4)))
    a = np.ascontiguousarray(np.asarray(inputs["L0_ffn_gate_a"]).T).reshape(NDT, 128, RANK)
    sh["ga_pk"] = np.ascontiguousarray(np.transpose(a, (1, 0, 2)))
    a = np.ascontiguousarray(np.asarray(inputs["L0_ffn_up_a"]).T).reshape(NDT, 128, RANK)
    sh["ua_pk"] = np.ascontiguousarray(np.transpose(a, (1, 0, 2)))
    sh["gbT"] = np.ascontiguousarray(np.asarray(inputs["L0_ffn_gate_b"]).T)
    sh["ubT"] = np.ascontiguousarray(np.asarray(inputs["L0_ffn_up_b"]).T)
    a = np.ascontiguousarray(np.asarray(inputs["L0_ffn_down_a"]).T).reshape(NJT, 128, RANK)
    sh["da_pk"] = np.ascontiguousarray(np.transpose(a, (1, 0, 2)))
    sh["dbT"] = np.ascontiguousarray(np.asarray(inputs["L0_ffn_down_b"]).T)
    sh["bq_pk"] = np.ascontiguousarray(b_qkv[qrows].reshape(NH, 128).T)
    sh["bk_pk"] = np.ascontiguousarray(b_qkv[krows].reshape(NKV, 128).T)
    sh["bv_pk"] = np.ascontiguousarray(b_qkv[vrows].reshape(1, 1024))
    sh["bd_pk"] = np.ascontiguousarray(np.asarray(inputs["L0_attn_b_dense"]).reshape(1, D))
    sh["mask_pk"] = _mask_consts()
    sh["ones_col"] = np.ones((128, 1), dtype=np.float32)
    sh["ones_colb"] = np.ones((128, 1), dtype=ml_dtypes.bfloat16)
    sh["ones_row"] = np.ones((1, 512), dtype=np.float32)

    in_maps = []
    for c in range(NCORES):
        packed, _, _ = _packed_blocks(c)
        colidx = np.concatenate([np.arange(max(b, 0) * 64, max(b, 0) * 64 + 64)
                                 for b in packed])
        m = dict(sh)
        m["xpk"] = np.ascontiguousarray(xT[:, colidx])
        m["cosk"], m["sink"] = _rope_tables(colidx)
        m["coso"], m["sino"] = _rope_tables(np.arange(ROWS * c, ROWS * (c + 1)))
        m["biaspat"] = _bias_pattern(c)
        in_maps.append(m)
    return in_maps


class _Sem:
    __slots__ = ("h", "n")

    def __init__(self, ctx, nc, name):
        self.h = ctx.enter_context(nc.semaphore(name))
        self.n = 0

    def inc(self, instr, k=1):
        instr.then_inc(self.h, k)
        self.n += k
        return instr


def _build(nc):
    dt = nc.dram_tensor
    sk = "ExternalOutput" if _DEBUG else "Internal"
    xpk = dt("xpk", [D, SKV], F32, kind="ExternalInput")
    wq_pk = dt("wq_pk", [NH, 128, NDT, 128], F32, kind="ExternalInput")
    wk_pk = dt("wk_pk", [NKV, 128, NDT, 128], F32, kind="ExternalInput")
    wv_pk = dt("wv_pk", [2, 128, NDT, 512], F32, kind="ExternalInput")
    wg_pk = dt("wg_pk", [NJT, 128, NDT, 128], F32, kind="ExternalInput")
    wu_pk = dt("wu_pk", [NJT, 128, NDT, 128], F32, kind="ExternalInput")
    wd_pk = dt("wd_pk", [8, 4, 128, 8, 512], F32, kind="ExternalInput")
    wn_pk = dt("wn_pk", [8, 22, 128, 4, 512], F32, kind="ExternalInput")
    ga_pk = dt("ga_pk", [128, NDT, RANK], F32, kind="ExternalInput")
    ua_pk = dt("ua_pk", [128, NDT, RANK], F32, kind="ExternalInput")
    gbT = dt("gbT", [RANK, DF], F32, kind="ExternalInput")
    ubT = dt("ubT", [RANK, DF], F32, kind="ExternalInput")
    da_pk = dt("da_pk", [128, NJT, RANK], F32, kind="ExternalInput")
    dbT = dt("dbT", [RANK, D], F32, kind="ExternalInput")
    bq_pk = dt("bq_pk", [128, NH], F32, kind="ExternalInput")
    bk_pk = dt("bk_pk", [128, NKV], F32, kind="ExternalInput")
    bv_pk = dt("bv_pk", [1, 1024], F32, kind="ExternalInput")
    bd_pk = dt("bd_pk", [1, D], F32, kind="ExternalInput")
    mask_pk = dt("mask_pk", [128, 8 * 512], F32, kind="ExternalInput")
    cosk = dt("cosk", [128, SKV], F32, kind="ExternalInput")
    sink = dt("sink", [128, SKV], F32, kind="ExternalInput")
    coso = dt("coso", [128, ROWS], F32, kind="ExternalInput")
    sino = dt("sino", [128, ROWS], F32, kind="ExternalInput")
    biaspat = dt("biaspat", [128, 2 * LU], F32, kind="ExternalInput")
    ones_col = dt("ones_col", [128, 1], F32, kind="ExternalInput")
    ones_colb = dt("ones_colb", [128, 1], BF16, kind="ExternalInput")
    ones_rowi = dt("ones_row", [1, 512], F32, kind="ExternalInput")
    xn_hbm = dt("xn_hbm", [D, SKV], BF16, kind=sk)
    q_hbm = dt("q_hbm", [D, ROWS], BF16, kind=sk)
    h_hbm = dt("h_hbm", [NJT, 128, ROWS], BF16, kind=sk)
    if _DEBUG:
        kv_dbg = dt("kv_dbg", [128, 49152], BF16, kind="ExternalOutput")
        ctx_dbg = dt("ctx_dbg", [128, 32768], BF16, kind="ExternalOutput")
    out_T = dt("out_T", [D, ROWS], F32, kind="ExternalOutput")

    ctx = ExitStack()
    with ctx:
        sb = lambda n, s, d: ctx.enter_context(nc.sbuf_tensor(n, s, d))
        resA = sb("resA", [128, 32768], BF16)
        arena = sb("arena", [128, 49152], BF16)
        scr = sb("scr", [128, 5120], BF16)
        mr = sb("mr", [128, 4096], F32)
        pool = sb("pool", [128, 7296], BF16)
        bpat = sb("bpat", [128, 2 * LU], F32)
        bq_sb = sb("bq_sb", [128, NH], F32)
        bk_sb = sb("bk_sb", [128, NKV], F32)
        tuu = sb("tuu", [128, 1536], BF16)
        onec_f = sb("onec_f", [128, 1], F32)
        onec_b = sb("onec_b", [128, 1], BF16)
        da_sb = sb("da_sb", [128, NJT * RANK], BF16)

        pA = ctx.enter_context(nc.psum_tensor("pA", [128, 1024], F32))
        pB = ctx.enter_context(nc.psum_tensor("pB", [128, 1024], F32))
        pC = ctx.enter_context(nc.psum_tensor("pC", [128, 1024], F32))
        pD = ctx.enter_context(nc.psum_tensor("pD", [128, 1024], F32))

        scrf = scr[:, :].bitcast(F32)
        mrb = mr[:, :].bitcast(BF16)
        poolf = pool[:, :].bitcast(F32)
        arenaf = arena[:, :].bitcast(F32)

        r_rstd = poolf[0:1, 0:512]
        r_mrs = poolf[0:1, 512:1024]
        r_ones = poolf[0:1, 1024:1536]
        r_den = poolf[0:1, 1536:2048]
        r_bv = poolf[0:1, 2048:3072]
        r_ksq = poolf[0:1, 3072:3080]
        r_sc = poolf[0:1, 3080:3088]
        p_bmat = poolf[:, 3088:3128]
        r_bd = poolf[0:1, 3128:3640]
        p_bcs = poolf[:, 0:512]
        p_mball = poolf[:, 512:576]
        r_m = mr[0:1, 1024:1536]
        r_es = mr[0:1, 1536:2048]
        r_m2 = mr[0:1, 3072:3584]
        r_var = mr[0:1, 3584:4096]
        r_ln = mr[0:1, 1536:2048]
        cos_o = arenaf[:, 0:1024]
        sin_o = arenaf[:, 1024:2048]

        mk = lambda name: _Sem(ctx, nc, name)
        block = ctx.enter_context(nc.Block())
        E = {}

        C0 = mk("C0")

        @block.gpsimd
        def _(g):
            C0.inc(g.dma_start(bpat[:, :], biaspat[:, :]), 16)
            C0.inc(g.dma_start(bq_sb[:, :], bq_pk[:, :]), 16)
            C0.inc(g.dma_start(bk_sb[:, :], bk_pk[:, :]), 16)
            C0.inc(g.dma_start(onec_f[:, :], ones_col[:, :]), 16)
            C0.inc(g.dma_start(onec_b[:, :], ones_colb[:, :]), 16)
            C0.inc(g.dma_start(r_ones, ones_rowi[:, :]), 16)
            C0.inc(g.dma_start(r_bv, bv_pk[:, :]), 16)
            C0.inc(g.dma_start(da_sb[:, :], da_pk[:, :].rearrange("p a b -> p (a b)")), 16)
        NC0 = C0.n

        if _PHASES >= 1:
            L1 = mk("L1")
            L1B = mk("L1B")
            SQ1 = mk("SQ1")
            TM1 = mk("TM1")
            VR1 = mk("VR1")
            SL1 = mk("SL1")
            VR2 = mk("VR2")
            BC1 = mk("BC1")
            VX1 = mk("VX1")
            XS1 = mk("XS1")
            strip = arenaf

            @block.sync
            def _(sy):
                for sbk in range(NSB):
                    if sbk >= 1:
                        sy.wait_ge(VX1.h, 64 * sbk)
                    src3 = xpk[:, sbk * 512:(sbk + 1) * 512].rearrange(
                        "(dt p) s -> p dt s", p=128)
                    dst3 = strip[:, 0:16384].rearrange("p (dt s) -> p dt s", s=512)
                    L1.inc(sy.dma_start(dst3, src3), 16)

            @block.gpsimd
            def _(g):
                for sbk in range(NSB):
                    if sbk >= 1:
                        g.wait_ge(TM1.h, 32 * sbk)
                    src3 = xpk[:, sbk * 512:(sbk + 1) * 512].rearrange(
                        "(dt p) s -> p dt s", p=128)
                    dst3 = arena[:, 32768:49152].rearrange(
                        "p (dt s) -> p dt s", s=512)
                    L1B.inc(g.dma_start(dst3, src3), 16)

            @block.scalar
            def _(sc):
                sc.wait_ge(C0.h, NC0)
                bstrip = arena[:, 32768:49152]
                for sbk in range(NSB):
                    sc.wait_ge(L1B.h, 16 * (sbk + 1))
                    for dti in range(NDT):
                        idx = sbk * NDT + dti
                        if idx >= 2:
                            sc.wait_ge(TM1.h, idx - 1)
                        slot = mrb[:, (dti % 2) * 512:(dti % 2) * 512 + 512]
                        SQ1.inc(sc.activation(
                            slot, bstrip[:, dti * 512:(dti + 1) * 512], AF.Square))
                    sc.wait_ge(VR1.h, 4 * sbk + 4)
                    SL1.inc(sc.activation(r_ln, r_m2, AF.Ln))
                    sc.wait_ge(SL1.h, 2 * sbk + 1)
                    SL1.inc(sc.activation(r_rstd, r_ln, AF.Exp, scale=-0.5))
                    own = sbk >= 4
                    for dti in range(NDT):
                        sc.wait_ge(VX1.h, 64 * sbk + 2 * dti + 2)
                        if own:
                            src = resA[:, dti * 1024 + (sbk - 4) * 512:
                                       dti * 1024 + (sbk - 4) * 512 + 512]
                        else:
                            so = (sbk * NDT + dti) % 4
                            src = scr[:, so * 512:(so + 1) * 512]
                        XS1.inc(sc.dma_start(
                            xn_hbm[dti * 128:(dti + 1) * 128,
                                   sbk * 512:(sbk + 1) * 512], src), 16)

            @block.tensor
            def _(te):
                te.wait_ge(C0.h, NC0)
                bstrip = arena[:, 32768:49152]
                for sbk in range(NSB):
                    te.wait_ge(L1.h, 16 * (sbk + 1))
                    te.wait_ge(L1B.h, 16 * (sbk + 1))
                    if sbk >= 1:
                        te.wait_ge(VR1.h, 4 * sbk)
                    for dti in range(NDT):
                        idx = sbk * NDT + dti
                        te.wait_ge(SQ1.h, idx + 1)
                        slot = mrb[:, (dti % 2) * 512:(dti % 2) * 512 + 512]
                        te.matmul(pA[0:1, 0:512], onec_b[:, :],
                                  bstrip[:, dti * 512:(dti + 1) * 512],
                                  start=(dti == 0), stop=(dti == NDT - 1))
                        m = te.matmul(pB[0:1, 0:512], onec_b[:, :], slot,
                                      start=(dti == 0), stop=(dti == NDT - 1))
                        TM1.inc(m, 1)
                    te.wait_ge(VR2.h, sbk + 1)
                    if sbk >= 1:
                        te.wait_ge(VX1.h, 64 * sbk)
                    te.matmul(pC[:, 0:512], r_ones[0:1, 0:128], r_rstd,
                              start=True, stop=True)
                    m = te.matmul(pC[:, 512:1024], r_ones[0:1, 0:128], r_mrs,
                                  start=True, stop=True)
                    BC1.inc(m, 1)

            @block.vector
            def _(ve):
                ve.wait_ge(C0.h, NC0)
                for sbk in range(NSB):
                    ve.wait_ge(TM1.h, 32 * (sbk + 1))
                    ve.tensor_scalar_mul(r_m, pA[0:1, 0:512], 1.0 / D)
                    VR1.inc(ve.tensor_scalar_mul(r_es, pB[0:1, 0:512], 1.0 / D))
                    ve.wait_ge(VR1.h, 4 * sbk + 1)
                    VR1.inc(ve.tensor_mul(r_m2, r_m, r_m))
                    ve.wait_ge(VR1.h, 4 * sbk + 2)
                    VR1.inc(ve.tensor_sub(r_var, r_es, r_m2))
                    ve.wait_ge(VR1.h, 4 * sbk + 3)
                    VR1.inc(ve.tensor_scalar_add(r_m2, r_var, 1e-5))
                    ve.wait_ge(SL1.h, 2 * sbk + 2)
                    VR2.inc(ve.tensor_mul(r_mrs, r_m, r_rstd))
                    ve.wait_ge(BC1.h, sbk + 1)
                    own = sbk >= 4
                    for dti in range(NDT):
                        tmp = mr[:, 2048 + (dti % 2) * 512:2048 + (dti % 2) * 512 + 512]
                        VX1.inc(ve.tensor_mul(
                            tmp, strip[:, dti * 512:(dti + 1) * 512], pC[:, 0:512]))
                        ve.wait_ge(VX1.h, 64 * sbk + 2 * dti + 1)
                        if own:
                            dst = resA[:, dti * 1024 + (sbk - 4) * 512:
                                       dti * 1024 + (sbk - 4) * 512 + 512]
                        else:
                            so = (sbk * NDT + dti) % 4
                            dst = scr[:, so * 512:(so + 1) * 512]
                            hidx = sbk * NDT + dti
                            if hidx >= 4:
                                ve.wait_ge(XS1.h, 16 * (hidx - 3))
                        VX1.inc(ve.tensor_sub(dst, tmp, pC[:, 512:1024]))

            E["xn_tiles"] = (VX1, VX1.n)
            E["xn_hbm"] = (XS1, XS1.n)

        if _PHASES >= 2:
            LW3 = mk("LW3")
            CT3 = mk("CT3")
            T3 = mk("T3")
            SE3 = mk("SE3")
            VR3 = mk("VR3")
            QS3 = mk("QS3")

            @block.gpsimd
            def _(g):
                g.wait_ge(TM1.h, 32 * NSB)
                for et in range(NH):
                    if et >= 2:
                        g.wait_ge(T3.h, et - 1)
                    src = wq_pk[et].rearrange("p a b -> p (a b)")
                    sl = arena[:, 32768 + (et % 2) * 4096:
                               32768 + (et % 2) * 4096 + 4096]
                    LW3.inc(g.dma_start(sl, src), 16)

            @block.tensor
            def _(te):
                te.wait_ge(E["xn_tiles"][0].h, E["xn_tiles"][1])
                for et in range(NH):
                    te.wait_ge(LW3.h, 16 * (et + 1))
                    if et >= 2:
                        te.wait_ge(SE3.h, et - 1)
                    pq = pA if et % 2 == 0 else pB
                    wsl = arena[:, 32768 + (et % 2) * 4096:
                                32768 + (et % 2) * 4096 + 4096]
                    for dti in range(NDT):
                        for sh2 in range(2):
                            m = te.matmul(pq[:, sh2 * 512:(sh2 + 1) * 512],
                                          wsl[:, dti * 128:(dti + 1) * 128],
                                          resA[:, dti * 1024 + sh2 * 512:
                                               dti * 1024 + sh2 * 512 + 512],
                                          start=(dti == 0), stop=(dti == NDT - 1))
                    T3.inc(m, 1)

            @block.scalar
            def _(sc):
                CT3.inc(sc.dma_start(cos_o, coso[:, :]), 16)
                CT3.inc(sc.dma_start(sin_o, sino[:, :]), 16)
                for et in range(NH):
                    sc.wait_ge(T3.h, et + 1)
                    if et >= 2:
                        sc.wait_ge(VR3.h, 5 * (et - 1))
                    pq = pA if et % 2 == 0 else pB
                    qf = mr[:, (et % 2) * 1024:(et % 2) * 1024 + 1024]
                    SE3.inc(sc.activation(qf, pq[:, :], AF.Identity,
                                          bias=bq_sb[:, et:et + 1]))
                    sc.wait_ge(VR3.h, 5 * (et + 1))
                    QS3.inc(sc.dma_start(
                        q_hbm[et * 128:(et + 1) * 128, :], scr[:, 4096:5120]), 16)

            @block.vector
            def _(ve):
                ve.wait_ge(CT3.h, 32)
                for et in range(NH):
                    ve.wait_ge(SE3.h, et + 1)
                    qf = mr[:, (et % 2) * 1024:(et % 2) * 1024 + 1024]
                    rr = mr[:, 2048:3072]
                    aa = mr[:, 3072:4096]
                    VR3.inc(ve.tensor_copy(aa[0:64, :], qf[64:128, :]))
                    VR3.inc(ve.tensor_copy(aa[64:128, :], qf[0:64, :]))
                    ve.wait_ge(VR3.h, 5 * et + 2)
                    VR3.inc(ve.tensor_mul(rr, aa, sin_o))
                    VR3.inc(ve.tensor_mul(aa, qf, cos_o))
                    ve.wait_ge(VR3.h, 5 * et + 4)
                    if et >= 1:
                        ve.wait_ge(QS3.h, 16 * et)
                    qo = scr[:, 4096:5120]
                    VR3.inc(ve.tensor_add(qo, aa, rr))

            E["q_hbm"] = (QS3, QS3.n)
            E["ph3_v"] = (VR3, VR3.n)

        if _PHASES >= 3:
            L4 = mk("L4")
            T4 = mk("T4")
            T4t = mk("T4t")
            S4 = mk("S4")
            V4 = mk("V4")
            HS4 = mk("HS4")
            U4 = mk("U4")

            @block.gpsimd
            def _(g):
                g.wait_ge(E["ph3_v"][0].h, E["ph3_v"][1])
                g.wait_ge(E["q_hbm"][0].h, E["q_hbm"][1])
                L4.inc(g.dma_start(scr[:, 4096:4608],
                                   ga_pk[:, :].rearrange("p a b -> p (a b)")), 16)
                L4.inc(g.dma_start(scr[:, 4608:5120],
                                   ua_pk[:, :].rearrange("p a b -> p (a b)")), 16)
                for jt in range(NJT):
                    if jt >= 2:
                        g.wait_ge(T4.h, 2 + 2 * (jt - 1))
                    elif jt >= 1:
                        g.wait_ge(T4.h, 2)
                    else:
                        g.wait_ge(T4.h, 2)
                    wga = arena[:, 12288 + (jt % 2) * 4096:
                                12288 + (jt % 2) * 4096 + 4096]
                    wua = arena[:, 20480 + (jt % 2) * 4096:
                                20480 + (jt % 2) * 4096 + 4096]
                    L4.inc(g.dma_start(wga,
                                       wg_pk[jt].rearrange("p a b -> p (a b)")), 16)
                    L4.inc(g.dma_start(wua,
                                       wu_pk[jt].rearrange("p a b -> p (a b)")), 16)
                    L4.inc(g.dma_start(scr[32:48, 4096 + (jt % 2) * 128:
                                           4096 + (jt % 2) * 128 + 128],
                                       gbT[:, jt * 128:(jt + 1) * 128]), 16)
                    L4.inc(g.dma_start(scr[64:80, 4352 + (jt % 2) * 128:
                                           4352 + (jt % 2) * 128 + 128],
                                       ubT[:, jt * 128:(jt + 1) * 128]), 16)

            @block.tensor
            def _(te):
                te.wait_ge(L4.h, 32)
                for which, lo in ((0, 4096), (1, 4608)):
                    if which == 1:
                        te.wait_ge(S4.h, 1)
                    for dti in range(NDT):
                        for sh2 in range(2):
                            m = te.matmul(pD[0:16, sh2 * 512:(sh2 + 1) * 512],
                                          scr[:, lo + dti * 16:lo + dti * 16 + 16],
                                          resA[:, dti * 1024 + sh2 * 512:
                                               dti * 1024 + sh2 * 512 + 512],
                                          start=(dti == 0), stop=(dti == NDT - 1))
                    T4.inc(m, 1)
                te.wait_ge(U4.h, 32)
                for jt in range(NJT):
                    te.wait_ge(L4.h, 32 + 64 * (jt + 1))
                    te.wait_ge(S4.h, 2 + jt)
                    wga = arena[:, 12288 + (jt % 2) * 4096:
                                12288 + (jt % 2) * 4096 + 4096]
                    wua = arena[:, 20480 + (jt % 2) * 4096:
                                20480 + (jt % 2) * 4096 + 4096]
                    for dti in range(NDT):
                        for sh2 in range(2):
                            te.matmul(pA[:, sh2 * 512:(sh2 + 1) * 512],
                                      wga[:, dti * 128:(dti + 1) * 128],
                                      resA[:, dti * 1024 + sh2 * 512:
                                           dti * 1024 + sh2 * 512 + 512],
                                      start=(dti == 0), stop=False)
                    for sh2 in range(2):
                        m = te.matmul(pA[:, sh2 * 512:(sh2 + 1) * 512],
                                      scr[32:48, 4096 + (jt % 2) * 128:
                                          4096 + (jt % 2) * 128 + 128],
                                      tuu[32:48, sh2 * 512:(sh2 + 1) * 512],
                                      start=False, stop=True)
                    T4.inc(m, 1)
                    if jt >= 1:
                        te.wait_ge(V4.h, jt)
                        hb = scr[:, 2048 + ((jt - 1) % 2) * 1024:
                                 2048 + ((jt - 1) % 2) * 1024 + 1024]
                        for sh2 in range(2):
                            m = te.matmul(pC[0:16, sh2 * 512:(sh2 + 1) * 512],
                                          da_sb[:, (jt - 1) * 16:jt * 16],
                                          hb[:, sh2 * 512:(sh2 + 1) * 512],
                                          start=(jt == 1), stop=False)
                        T4t.inc(m, 1)
                    te.wait_ge(V4.h, jt)
                    for dti in range(NDT):
                        for sh2 in range(2):
                            te.matmul(pB[:, sh2 * 512:(sh2 + 1) * 512],
                                      wua[:, dti * 128:(dti + 1) * 128],
                                      resA[:, dti * 1024 + sh2 * 512:
                                           dti * 1024 + sh2 * 512 + 512],
                                      start=(dti == 0), stop=False)
                    for sh2 in range(2):
                        m = te.matmul(pB[:, sh2 * 512:(sh2 + 1) * 512],
                                      scr[64:80, 4352 + (jt % 2) * 128:
                                          4352 + (jt % 2) * 128 + 128],
                                      tuu[64:80, sh2 * 512:(sh2 + 1) * 512],
                                      start=False, stop=True)
                    T4.inc(m, 1)
                te.wait_ge(V4.h, NJT)
                hb = scr[:, 2048 + ((NJT - 1) % 2) * 1024:
                         2048 + ((NJT - 1) % 2) * 1024 + 1024]
                for sh2 in range(2):
                    m = te.matmul(pC[0:16, sh2 * 512:(sh2 + 1) * 512],
                                  da_sb[:, (NJT - 1) * 16:NJT * 16],
                                  hb[:, sh2 * 512:(sh2 + 1) * 512],
                                  start=False, stop=True)
                T4t.inc(m, 1)

            @block.scalar
            def _(sc):
                for which, dst in ((0, 32), (1, 64)):
                    sc.wait_ge(T4.h, which + 1)
                    if which == 1:
                        sc.wait_ge(U4.h, 16)
                    S4.inc(sc.activation(scr[0:16, 2048:3072], pD[0:16, :], AF.Copy))
                    sc.wait_ge(S4.h, which + 1)
                    U4.inc(sc.dma_start(tuu[dst:dst + 16, 0:1024],
                                        scr[0:16, 2048:3072]), 16)
                for jt in range(NJT):
                    sc.wait_ge(T4.h, 2 + 2 * jt + 1)
                    if jt >= 1:
                        sc.wait_ge(V4.h, jt)
                    S4.inc(sc.activation(scrf[:, 0:1024], pA[:, :], AF.Silu))
                    if jt >= 1:
                        hb = scr[:, 2048 + ((jt - 1) % 2) * 1024:
                                 2048 + ((jt - 1) % 2) * 1024 + 1024]
                        HS4.inc(sc.dma_start(h_hbm[jt - 1], hb), 16)
                sc.wait_ge(V4.h, NJT)
                hb = scr[:, 2048 + ((NJT - 1) % 2) * 1024:
                         2048 + ((NJT - 1) % 2) * 1024 + 1024]
                HS4.inc(sc.dma_start(h_hbm[NJT - 1], hb), 16)
                sc.wait_ge(T4t.h, NJT)
                S4.inc(sc.activation(tuu[0:16, 0:1024], pC[0:16, :], AF.Copy))

            @block.vector
            def _(ve):
                for jt in range(NJT):
                    ve.wait_ge(S4.h, 2 + jt + 1)
                    ve.wait_ge(T4.h, 2 + 2 * jt + 2)
                    if jt >= 2:
                        ve.wait_ge(HS4.h, 16 * (jt - 1))
                        ve.wait_ge(T4t.h, jt - 1)
                    hb = scr[:, 2048 + (jt % 2) * 1024:2048 + (jt % 2) * 1024 + 1024]
                    V4.inc(ve.tensor_mul(hb, scrf[:, 0:1024], pB[:, :]))

            E["h_hbm"] = (HS4, HS4.n)
            E["t_sb"] = (S4, S4.n)
            E["ph4_t"] = (T4, T4.n)
            E["ph4_v"] = (V4, V4.n)

        if _PHASES >= 4:
            L2 = mk("L2")
            LS2 = mk("LS2")
            KT2 = mk("KT2")
            T2 = mk("T2")
            S2 = mk("S2")
            V2 = mk("V2")
            BV2 = mk("BV2")
            KS2 = mk("KS2")
            passes = [("k", 0), ("k", 1), ("v", 0), ("v", 1)]

            @block.gpsimd
            def _(g):
                g.wait_ge(E["h_hbm"][0].h, E["h_hbm"][1])
                g.wait_ge(E["ph4_v"][0].h, E["ph4_v"][1])
                for el in range(4):
                    src = wk_pk[el].rearrange("p a b -> p (a b)")
                    L2.inc(g.dma_start(
                        resA[:, 16384 + el * 4096:16384 + (el + 1) * 4096],
                        src), 16)
                for el in range(4):
                    src = wk_pk[4 + el].rearrange("p a b -> p (a b)")
                    L2.inc(g.dma_start(
                        arena[:, 24576 + el * 4096:24576 + (el + 1) * 4096],
                        src), 16)
                g.wait_ge(T2.h, 24)
                src = wv_pk[0].rearrange("p a b -> p (a b)")
                L2.inc(g.dma_start(resA[:, 16384:32768], src), 16)
                g.wait_ge(T2.h, 72)
                src = wv_pk[1].rearrange("p a b -> p (a b)")
                L2.inc(g.dma_start(resA[:, 16384:32768], src), 16)

            @block.sync
            def _(sy):
                sy.wait_ge(E["xn_hbm"][0].h, E["xn_hbm"][1])
                sy.wait_ge(E["ph4_t"][0].h, E["ph4_t"][1])
                for pi in range(4):
                    for sbk in range(NSB):
                        idx = pi * NSB + sbk
                        if idx >= 1:
                            sy.wait_ge(T2.h, 4 * idx)
                        for q in range(4):
                            src3 = xn_hbm[1024 * q:1024 * (q + 1),
                                          sbk * 512:(sbk + 1) * 512].rearrange(
                                "(dt p) s -> p dt s", p=128)
                            dst3 = resA[:, q * 4096:(q + 1) * 4096].rearrange(
                                "p (dt s) -> p dt s", s=512)
                            LS2.inc(sy.dma_start(dst3, src3), 16)

            LW_CUM = [4, 8, 9, 10]

            @block.tensor
            def _(te):
                te.wait_ge(E["t_sb"][0].h, E["t_sb"][1])
                whome = {0: resA[:, 16384:32768], 1: arena[:, 24576:40960],
                         2: resA[:, 16384:32768], 3: resA[:, 16384:32768]}
                for pi, (kind, half) in enumerate(passes):
                    te.wait_ge(L2.h, 16 * LW_CUM[pi])
                    wh = whome[pi]
                    if kind == "v":
                        m = te.matmul(pD[:, 0:512], r_ones[0:1, 0:128],
                                      r_bv[0:1, half * 512:half * 512 + 512],
                                      start=True, stop=True)
                        BV2.inc(m, 1)
                    for sbk in range(NSB):
                        idx = pi * NSB + sbk
                        stp = resA[:, 0:16384]
                        if kind == "k":
                            for el in range(4):
                                tix = idx * 4 + el
                                if tix >= 4:
                                    te.wait_ge(V2.h, 6 * (tix - 4) + 1)
                                ps = (pA, pB)[el // 2][:, (el % 2) * 512:
                                                       (el % 2) * 512 + 512]
                                for dti in range(NDT):
                                    if el == 0 and dti % 8 == 0:
                                        te.wait_ge(LS2.h,
                                                   16 * (4 * idx + dti // 8 + 1))
                                    m = te.matmul(
                                        ps, wh[:, (el * NDT + dti) * 128:
                                               (el * NDT + dti) * 128 + 128],
                                        stp[:, dti * 512:(dti + 1) * 512],
                                        start=(dti == 0), stop=(dti == NDT - 1))
                                T2.inc(m, 1)
                        else:
                            for st in range(4):
                                tix = ((pi - 2) * NSB + sbk) * 4 + st
                                if tix >= 2:
                                    te.wait_ge(V2.h, 288 + tix - 1)
                                ps = pC[:, (st % 2) * 512:(st % 2) * 512 + 512]
                                for dti in range(NDT):
                                    if st == 0 and dti % 8 == 0:
                                        te.wait_ge(LS2.h,
                                                   16 * (4 * idx + dti // 8 + 1))
                                    m = te.matmul(
                                        ps, stp[:, dti * 512 + st * 128:
                                                dti * 512 + st * 128 + 128],
                                        wh[:, dti * 512:(dti + 1) * 512],
                                        start=(dti == 0), stop=(dti == NDT - 1))
                                T2.inc(m, 1)
                for g8 in range(NKV):
                    for sbk in range(NSB):
                        i2 = g8 * NSB + sbk
                        te.wait_ge(S2.h, 2 + i2 + 1)
                        if i2 >= 1:
                            te.wait_ge(KS2.h, 3 * i2 - 1)
                        m = te.matmul(pD[0:1, 512:1024], onec_f[:, :],
                                      mr[:, 0:512], start=True, stop=True)
                        KS2.inc(m, 1)

            @block.scalar
            def _(sc):
                for pi in range(2):
                    for sbk in range(NSB):
                        kidx = pi * NSB + sbk
                        if kidx >= 2:
                            sc.wait_ge(V2.h, 24 * (kidx - 1))
                        co = mr[:, 1024 + (sbk % 2) * 512:1024 + (sbk % 2) * 512 + 512]
                        si = mr[:, 2048 + (sbk % 2) * 512:2048 + (sbk % 2) * 512 + 512]
                        KT2.inc(sc.dma_start(co, cosk[:, sbk * 512:(sbk + 1) * 512]), 16)
                        KT2.inc(sc.dma_start(si, sink[:, sbk * 512:(sbk + 1) * 512]), 16)
                for half in range(2):
                    sc.wait_ge(BV2.h, half + 1)
                    sc.wait_ge(V2.h, 288 if half == 0 else 288 + 24)
                    S2.inc(sc.activation(mr[:, 3072:3584], pD[:, 0:512], AF.Copy))
                for g8 in range(NKV):
                    for sbk in range(NSB):
                        i2 = g8 * NSB + sbk
                        if i2 >= 1:
                            sc.wait_ge(KS2.h, 3 * i2 - 2)
                        S2.inc(sc.activation(
                            mr[:, 0:512],
                            arena[:, g8 * 3072 + sbk * 512:g8 * 3072 + sbk * 512 + 512],
                            AF.Square))

            @block.vector
            def _(ve):
                for pi in range(2):
                    for sbk in range(NSB):
                        kidx = pi * NSB + sbk
                        for el in range(4):
                            tix = kidx * 4 + el
                            ve.wait_ge(T2.h, tix + 1)
                            ve.wait_ge(KT2.h, 32 * (kidx + 1))
                            g8 = 4 * pi + el
                            ps = (pA, pB)[el // 2][:, (el % 2) * 512:
                                                   (el % 2) * 512 + 512]
                            kev = mr[:, (tix % 2) * 512:(tix % 2) * 512 + 512]
                            co = mr[:, 1024 + (sbk % 2) * 512:1024 + (sbk % 2) * 512 + 512]
                            si = mr[:, 2048 + (sbk % 2) * 512:2048 + (sbk % 2) * 512 + 512]
                            rr = mr[:, 3072:3584]
                            aa = mr[:, 3584:4096]
                            V2.inc(ve.tensor_scalar_add(kev, ps, bk_sb[:, g8:g8 + 1]))
                            ve.wait_ge(V2.h, 6 * tix + 1)
                            V2.inc(ve.tensor_copy(aa[0:64, :], kev[64:128, :]))
                            V2.inc(ve.tensor_copy(aa[64:128, :], kev[0:64, :]))
                            ve.wait_ge(V2.h, 6 * tix + 3)
                            V2.inc(ve.tensor_mul(rr, aa, si))
                            V2.inc(ve.tensor_mul(aa, kev, co))
                            ve.wait_ge(V2.h, 6 * tix + 5)
                            dst = arena[:, g8 * 3072 + sbk * 512:
                                        g8 * 3072 + sbk * 512 + 512]
                            V2.inc(ve.tensor_add(dst, aa, rr))
                for pi in range(2):
                    ve.wait_ge(S2.h, pi + 1)
                    for sbk in range(NSB):
                        for st in range(4):
                            tix = (pi * NSB + sbk) * 4 + st
                            ve.wait_ge(T2.h, 48 + tix + 1)
                            ps = pC[:, (st % 2) * 512:(st % 2) * 512 + 512]
                            vt = sbk * 4 + st
                            dst = arena[:, 24576 + vt * 1024 + pi * 512:
                                        24576 + vt * 1024 + pi * 512 + 512]
                            V2.inc(ve.tensor_add(dst, ps, mr[:, 3072:3584]))
                for g8 in range(NKV):
                    for sbk in range(NSB):
                        i2 = g8 * NSB + sbk
                        ve.wait_ge(KS2.h, 3 * i2 + 1)
                        KS2.inc(ve.reduce_max(r_sc[0:1, 0:1], pD[0:1, 512:1024],
                                              axis=mybir.AxisListType.X))
                        ve.wait_ge(KS2.h, 3 * i2 + 2)
                        if sbk == 0:
                            KS2.inc(ve.tensor_copy(r_ksq[0:1, g8:g8 + 1],
                                                   r_sc[0:1, 0:1]))
                        else:
                            KS2.inc(ve.tensor_max(r_ksq[0:1, g8:g8 + 1],
                                                  r_ksq[0:1, g8:g8 + 1],
                                                  r_sc[0:1, 0:1]))

            E["ph2_t"] = (T2, T2.n)
            E["ph2_s"] = (S2, S2.n)
            E["ph2_v"] = (V2, V2.n)
            E["ph2_ks"] = (KS2, KS2.n)

        if _PHASES >= 5:
            M5 = mk("M5")
            Q5 = mk("Q5")
            S5p = mk("S5p")
            T5p = mk("T5p")
            V5p = mk("V5p")
            S5q = mk("S5q")
            V5b = mk("V5b")
            T5s = mk("T5s")
            V5e = mk("V5e")
            S5e = mk("S5e")
            T5c = mk("T5c")
            V5r = mk("V5r")
            T5d = mk("T5d")
            S5d = mk("S5d")
            V5c = mk("V5c")

            e_in = (scrf[:, 1536:2048], scrf[:, 2048:2560])
            sqs = scrf[:, 1536:2048]
            iters = [(qt, g8, i) for qt in range(2) for g8 in range(NKV)
                     for i in range(4)]
            NIT = len(iters)

            def esl4(uix):
                return scr[:, 1024 + (uix % 4) * 512:1024 + (uix % 4) * 512 + 512]

            def psc3(uix):
                return (pA[:, 0:512], pA[:, 512:1024],
                        pB[:, 512:1024], pD[:, 0:512])[uix % 4]

            def qslot(it2):
                return scr[:, (it2 % 2) * 512:(it2 % 2) * 512 + 512]

            @block.gpsimd
            def _(g):
                g.wait_ge(E["ph2_v"][0].h, E["ph2_v"][1])
                g.wait_ge(E["ph2_ks"][0].h, E["ph2_ks"][1])
                M5.inc(g.dma_start(mr[:, :], mask_pk[:, :]), 16)

            @block.scalar
            def _(sc):
                sc.wait_ge(E["q_hbm"][0].h, E["q_hbm"][1])
                sc.wait_ge(E["ph2_t"][0].h, E["ph2_t"][1])
                sc.wait_ge(E["ph2_ks"][0].h, E["ph2_ks"][1])
                for it, (qt, g8, i) in enumerate(iters):
                    h = 4 * g8 + i
                    if it >= 2:
                        sc.wait_ge(S5p.h, it - 1)
                    Q5.inc(sc.dma_start(
                        qslot(it), q_hbm[h * 128:(h + 1) * 128,
                                         qt * 512:qt * 512 + 512]), 16)
                    sc.wait_ge(Q5.h, 16 * (it + 1))
                    if it >= 1:
                        sc.wait_ge(T5p.h, 2 * it - 1)
                    S5p.inc(sc.activation(sqs, qslot(it), AF.Square))
                    if it >= 1:
                        sc.wait_ge(T5p.h, 2 * it)
                        S5q.inc(sc.activation(p_mball[:, it - 1:it],
                                              pD[:, 0:1], AF.Copy,
                                              scale=1.0 / 256))
                sc.wait_ge(T5p.h, 2 * NIT)
                S5q.inc(sc.activation(p_mball[:, NIT - 1:NIT],
                                      pD[:, 0:1], AF.Copy, scale=1.0 / 256))

            @block.tensor
            def _(te):
                te.wait_ge(E["ph2_s"][0].h, E["ph2_s"][1])
                te.wait_ge(E["ph2_ks"][0].h, E["ph2_ks"][1])
                te.wait_ge(E["ph2_v"][0].h, E["ph2_v"][1])
                for it in range(NIT):
                    te.wait_ge(S5p.h, it + 1)
                    if it >= 1:
                        te.wait_ge(V5p.h, 2 * it)
                    m = te.matmul(pC[0:1, 512:1024], onec_f[:, :], sqs,
                                  start=True, stop=True)
                    T5p.inc(m, 1)
                    te.wait_ge(V5p.h, 2 * (it + 1))
                    if it >= 1:
                        te.wait_ge(S5q.h, it)
                    m = te.matmul(pD[:, 0:1], r_ones[0:1, 0:128],
                                  r_sc[0:1, 1:2], start=True, stop=True)
                    T5p.inc(m, 1)

            @block.vector
            def _(ve):
                ve.wait_ge(M5.h, 16)
                for it, (qt, g8, i) in enumerate(iters):
                    ve.wait_ge(T5p.h, 2 * it + 1)
                    V5p.inc(ve.reduce_max(r_sc[0:1, 0:1], pC[0:1, 512:1024],
                                          axis=mybir.AxisListType.X))
                    ve.wait_ge(V5p.h, 2 * it + 1)
                    if it >= 1:
                        ve.wait_ge(T5p.h, 2 * it)
                    V5p.inc(ve.tensor_add(r_sc[0:1, 1:2], r_sc[0:1, 0:1],
                                          r_ksq[0:1, g8:g8 + 1]))

            @block.scalar
            def _(sc):
                def qmload(it2):
                    if it2 >= 2:
                        sc.wait_ge(T5s.h, LU * (it2 - 1))
                    h2 = 4 * iters[it2][1] + iters[it2][2]
                    Q5.inc(sc.dma_start(
                        qslot(it2), q_hbm[h2 * 128:(h2 + 1) * 128,
                                          iters[it2][0] * 512:
                                          iters[it2][0] * 512 + 512]), 16)

                qmload(0)
                qmload(1)
                for it, (qt, g8, i) in enumerate(iters):
                    h = 4 * g8 + i
                    for u in range(LU):
                        uix = it * LU + u
                        sc.wait_ge(T5s.h, uix + 1)
                        sc.wait_ge(V5b.h, it + 1)
                        if uix >= 4:
                            sc.wait_ge(T5c.h, 2 * (uix - 3))
                        if u in MASK_SLOT:
                            sc.wait_ge(V5e.h, 8 * it + MASK_SLOT[u] + 1)
                            srcap = e_in[u % 2]
                        else:
                            srcap = psc3(uix)
                        S5e.inc(sc.activation(esl4(uix), srcap, AF.Exp, scale=SM,
                                              bias=p_bmat[:, (it % 2) * 20 + u:
                                                          (it % 2) * 20 + u + 1]))
                    sc.wait_ge(T5d.h, it + 1)
                    if it >= 1:
                        sc.wait_ge(V5c.h, it)
                    S5d.inc(sc.activation(p_bcs, pD[:, 512:1024], AF.Copy))
                    if it + 2 < NIT:
                        qmload(it + 2)

            @block.tensor
            def _(te):
                te.wait_ge(S5q.h, NIT)
                for it, (qt, g8, i) in enumerate(iters):
                    qsl = qslot(it)
                    te.wait_ge(Q5.h, 16 * (NIT + it + 1))
                    for u in range(LU + 2):
                        if u < LU:
                            uix = it * LU + u
                            kt = _unit_kt(qt, u)
                            if uix >= 4:
                                pix = uix - 4
                                te.wait_ge(S5e.h, pix + 1)
                                pu = pix % LU
                                if pu in MASK_SLOT:
                                    te.wait_ge(V5e.h, 8 * (pix // LU) +
                                               MASK_SLOT[pu] + 1)
                            m = te.matmul(psc3(uix),
                                          arena[:, g8 * 3072 + kt * 128:
                                                g8 * 3072 + kt * 128 + 128],
                                          qsl, start=True, stop=True)
                            T5s.inc(m, 1)
                        if u >= 2:
                            v = u - 2
                            vix = it * LU + v
                            ktv = _unit_kt(qt, v)
                            te.wait_ge(S5e.h, vix + 1)
                            if v == 0:
                                te.wait_ge(V5c.h, it)
                                te.wait_ge(V5r.h, it)
                            te.matmul(pB[:, 0:512],
                                      arena[:, 24576 + ktv * 1024 + g8 * 128:
                                            24576 + ktv * 1024 + g8 * 128 + 128],
                                      esl4(vix), start=(v == 0), stop=(v == LU - 1))
                            m = te.matmul(pC[0:1, 0:512], onec_b[:, :], esl4(vix),
                                          start=(v == 0), stop=(v == LU - 1))
                            T5c.inc(m, 2)
                    te.wait_ge(V5r.h, it + 1)
                    m = te.matmul(pD[:, 512:1024], r_ones[0:1, 0:128], r_den,
                                  start=True, stop=True)
                    T5d.inc(m, 1)

            @block.vector
            def _(ve):
                for it, (qt, g8, i) in enumerate(iters):
                    ve.wait_ge(S5q.h, it + 1)
                    if it >= 2:
                        ve.wait_ge(S5e.h, LU * (it - 1))
                    V5b.inc(ve.tensor_scalar(
                        p_bmat[:, (it % 2) * 20:(it % 2) * 20 + 20],
                        bpat[:, qt * LU:qt * LU + LU],
                        p_mball[:, it:it + 1], None, ALU.subtract))
                    ve.wait_ge(S5e.h, LU * it)
                    prev_same_slot = {10: 8, 11: 9, 16: 10, 17: 11, 18: 16, 19: 17}
                    for u in MASK_UNITS:
                        uix = it * LU + u
                        ve.wait_ge(T5s.h, uix + 1)
                        if u in prev_same_slot:
                            ve.wait_ge(S5e.h, it * LU + prev_same_slot[u] + 1)
                        slot = MASK_SLOT[u]
                        V5e.inc(ve.tensor_add(
                            e_in[u % 2], psc3(uix),
                            mr[:, slot * 512:(slot + 1) * 512]))
                    ve.wait_ge(T5c.h, 2 * LU * (it + 1))
                    ve.wait_ge(T5d.h, it)
                    V5r.inc(ve.reciprocal(r_den, pC[0:1, 0:512]))
                    ve.wait_ge(S5d.h, it + 1)
                    h = 4 * g8 + i
                    dst = resA[:, h * 1024 + qt * 512:h * 1024 + qt * 512 + 512]
                    V5c.inc(ve.tensor_mul(dst, pB[:, 0:512], p_bcs))

            E["ctx"] = (V5c, V5c.n)
            E["ph5_se"] = (S5e, S5e.n)
            E["ph5_vr"] = (V5r, V5r.n)
            E["ph5_sd"] = (S5d, S5d.n)
            E["ph5_tc"] = (T5c, T5c.n)

        if _PHASES >= 6:
            L6 = mk("L6")
            H6 = mk("H6")
            X6 = mk("X6")
            B6 = mk("B6")
            T6d = mk("T6d")
            T6n = mk("T6n")
            V6 = mk("V6")
            OUT = mk("OUT")
            psl = (pA, pB, pC, pD)

            @block.gpsimd
            def _(g):
                g.wait_ge(E["ph5_se"][0].h, E["ph5_se"][1])
                g.wait_ge(E["ctx"][0].h, E["ctx"][1])
                g.wait_ge(E["ph5_tc"][0].h, E["ph5_tc"][1])
                for og in range(8):
                    if og >= 1:
                        g.wait_ge(T6d.h, 4 * og)
                    L6.inc(g.dma_start(tuu[0:16, 1024:1536],
                                       dbT[:, og * 512:og * 512 + 512]), 16)
                    for dtg in range(4):
                        dix = og * 4 + dtg
                        if dix >= 2:
                            g.wait_ge(T6d.h, dix - 1)
                        src = wd_pk[og, dtg].rearrange("p a b -> p (a b)")
                        sl = arena[:, (dix % 2) * 4096:(dix % 2) * 4096 + 4096]
                        L6.inc(g.dma_start(sl, src), 16)
                    for cc in range(22):
                        cix = og * 22 + cc
                        if cix >= 2:
                            pog, pcc = divmod(cix - 2, 22)
                            g.wait_ge(T6n.h, pog * 43 + min(43, 2 * pcc + 2))
                        src = wn_pk[og, cc].rearrange("p a b -> p (a b)")
                        sl = arena[:, 8192 + (cix % 2) * 2048:
                                   8192 + (cix % 2) * 2048 + 2048]
                        L6.inc(g.dma_start(sl, src), 16)

            @block.sync
            def _(sy):
                sy.wait_ge(E["h_hbm"][0].h, E["h_hbm"][1])
                sy.wait_ge(E["ph5_se"][0].h, E["ph5_se"][1])
                sy.wait_ge(E["ph5_sd"][0].h, E["ph5_sd"][1])
                for og in range(8):
                    if og >= 1:
                        sy.wait_ge(T6d.h, 4 * og)
                    B6.inc(sy.dma_start(r_bd[0:1, 0:512],
                                        bd_pk[:, og * 512:og * 512 + 512]), 16)
                    for hc in range(43):
                        if hc >= 2:
                            sy.wait_ge(T6n.h, og * 43 + hc - 1)
                        elif og > 0:
                            sy.wait_ge(T6n.h, og * 43)
                        hb = mrb[:, (hc % 2) * 2048:(hc % 2) * 2048 + 2048]
                        src = h_hbm[2 * hc:2 * hc + 2].rearrange("a p s -> p a s")
                        dst = hb.rearrange("p (a s) -> p a s", s=1024)
                        H6.inc(sy.dma_start(dst, src), 16)
                    for ot in range(4):
                        oix = og * 4 + ot
                        if oix >= 1:
                            sy.wait_ge(V6.h, 2 * oix)
                        X6.inc(sy.dma_start(
                            mr[:, 2048:3072],
                            xpk[(og * 4 + ot) * 128:(og * 4 + ot) * 128 + 128,
                                2048:3072]), 16)

            @block.tensor
            def _(te):
                te.wait_ge(E["ctx"][0].h, E["ctx"][1])
                te.wait_ge(E["ph5_vr"][0].h, E["ph5_vr"][1])
                te.wait_ge(E["ph5_sd"][0].h, E["ph5_sd"][1])
                te.wait_ge(E["ph5_se"][0].h, E["ph5_se"][1])
                for og in range(8):
                    te.wait_ge(B6.h, 16 * (og + 1))
                    te.wait_ge(L6.h, 16 * (og * 27 + 1))
                    if og >= 1:
                        te.wait_ge(V6.h, 8 * og)
                    for ot in range(4):
                        for sh2 in range(2):
                            te.matmul(psl[ot][:, sh2 * 512:(sh2 + 1) * 512],
                                      r_bd[0:1, ot * 128:ot * 128 + 128],
                                      r_ones, start=True, stop=False)
                            te.matmul(psl[ot][:, sh2 * 512:(sh2 + 1) * 512],
                                      tuu[0:16, 1024 + ot * 128:1024 + ot * 128 + 128],
                                      tuu[0:16, sh2 * 512:(sh2 + 1) * 512],
                                      start=False, stop=False)
                    for dtg in range(4):
                        dix = og * 4 + dtg
                        te.wait_ge(L6.h, 16 * (og * 27 + 2 + dtg))
                        wsl = arena[:, (dix % 2) * 4096:(dix % 2) * 4096 + 4096]
                        for ot in range(4):
                            for di in range(8):
                                dti = dtg * 8 + di
                                for sh2 in range(2):
                                    m = te.matmul(
                                        psl[ot][:, sh2 * 512:(sh2 + 1) * 512],
                                        wsl[:, (di * 4 + ot) * 128:
                                            (di * 4 + ot) * 128 + 128],
                                        resA[:, dti * 1024 + sh2 * 512:
                                             dti * 1024 + sh2 * 512 + 512],
                                        start=False, stop=False)
                        T6d.inc(m, 1)
                    for hc in range(43):
                        cc = hc // 2
                        cix = og * 22 + cc
                        te.wait_ge(L6.h, 16 * (og * 27 + 6 + cc))
                        te.wait_ge(H6.h, 16 * (og * 43 + hc + 1))
                        wsl = arena[:, 8192 + (cix % 2) * 2048:
                                    8192 + (cix % 2) * 2048 + 2048]
                        hbp = mrb[:, (hc % 2) * 2048:(hc % 2) * 2048 + 2048]
                        last_hc = (hc == 42)
                        for ot in range(4):
                            for jin in range(2):
                                jt = 2 * hc + jin
                                jt_in = jt - 4 * cc
                                hb = hbp[:, jin * 1024:jin * 1024 + 1024]
                                for sh2 in range(2):
                                    m = te.matmul(
                                        psl[ot][:, sh2 * 512:(sh2 + 1) * 512],
                                        wsl[:, (jt_in * 4 + ot) * 128:
                                            (jt_in * 4 + ot) * 128 + 128],
                                        hb[:, sh2 * 512:(sh2 + 1) * 512],
                                        start=False, stop=(last_hc and jin == 1))
                        T6n.inc(m, 1)

            @block.vector
            def _(ve):
                for og in range(8):
                    ve.wait_ge(T6n.h, 43 * (og + 1))
                    for ot in range(4):
                        oix = og * 4 + ot
                        ve.wait_ge(X6.h, 16 * (oix + 1))
                        for sh2 in range(2):
                            oix2 = 2 * oix + sh2
                            if oix2 >= 2:
                                ve.wait_ge(OUT.h, 16 * (oix2 - 1))
                            osl = mr[:, 3072 + (oix2 % 2) * 512:
                                     3072 + (oix2 % 2) * 512 + 512]
                            V6.inc(ve.tensor_add(
                                osl, psl[ot][:, sh2 * 512:(sh2 + 1) * 512],
                                mr[:, 2048 + sh2 * 512:2048 + sh2 * 512 + 512]))

            @block.scalar
            def _(sc):
                for og in range(8):
                    for ot in range(4):
                        for sh2 in range(2):
                            oix2 = (og * 4 + ot) * 2 + sh2
                            sc.wait_ge(V6.h, oix2 + 1)
                            osl = mr[:, 3072 + (oix2 % 2) * 512:
                                     3072 + (oix2 % 2) * 512 + 512]
                            OUT.inc(sc.dma_start(
                                out_T[(og * 4 + ot) * 128:
                                      (og * 4 + ot) * 128 + 128,
                                      sh2 * 512:(sh2 + 1) * 512], osl), 16)
                sc.wait_ge(OUT.h, 1024)

        if _DEBUG and _PHASES >= 4:
            DBG = mk("DBG")

            @block.sync
            def _(sy):
                sy.wait_ge(E["ph2_v"][0].h, E["ph2_v"][1])
                DBG.inc(sy.dma_start(kv_dbg[:, :], arena[:, :]), 16)
                if _PHASES >= 5:
                    sy.wait_ge(E["ctx"][0].h, E["ctx"][1])
                    DBG.inc(sy.dma_start(ctx_dbg[:, :], resA[:, :]), 16)
                sy.wait_ge(DBG.h, 16 * (2 if _PHASES >= 5 else 1))

        if _PHASES < 6:
            FW = mk("FW")

            @block.sync
            def _(sy):
                if "ctx" in E:
                    sy.wait_ge(E["ctx"][0].h, E["ctx"][1])
                elif "ph2_v" in E:
                    sy.wait_ge(E["ph2_v"][0].h, E["ph2_v"][1])
                elif "h_hbm" in E:
                    sy.wait_ge(E["h_hbm"][0].h, E["h_hbm"][1])
                elif "q_hbm" in E:
                    sy.wait_ge(E["q_hbm"][0].h, E["q_hbm"][1])
                elif "xn_hbm" in E:
                    sy.wait_ge(E["xn_hbm"][0].h, E["xn_hbm"][1])
                FW.inc(sy.dma_start(out_T[0:128, 0:512], mr[:, 0:512]), 16)
                sy.wait_ge(FW.h, 16)

    return nc


def kernel(**inputs):
    in_maps = _pack_inputs(inputs)
    nc = bass.Bass("TRN2", target_bir_lowering=False)
    _build(nc)
    res = run_bass_kernel_spmd(nc, in_maps, core_ids=list(range(NCORES)))
    _LAST["res"] = res
    outs = [np.asarray(r["out_T"], dtype=np.float32) for r in res.results]
    full = np.concatenate(outs, axis=1)
    return np.ascontiguousarray(full.T).reshape(1, SEQ, D).astype(np.float32)
